# version 05
# baseline (speedup 1.0000x reference)
import numpy as np
from contextlib import ExitStack

import concourse.bass as bass
import concourse.mybir as mybir
from concourse.bass_utils import run_bass_kernel_spmd

F32 = mybir.dt.float32
BF16 = mybir.dt.bfloat16
U32 = mybir.dt.uint32
AF = mybir.ActivationFunctionType
OP = mybir.AluOpType
AX = mybir.AxisListType

NCORES = 8
NSC = 4
TSC = 512
DM = 1024
DI = 2048
NST = 16
DTR = 64
DSL = 512
NEXP = 8
FF = 4096
GT = 4 * TSC
GA = 8 * TSC
EPS = 1e-5

ENGS = ("sync", "scalar", "vector", "tensor", "gpsimd")
GROUP4 = [[0, 1, 2, 3], [4, 5, 6, 7]]
GROUP8 = [list(range(8))]


class SemC:
    __slots__ = ("h", "v")

    def __init__(self, handle):
        self.h = handle
        self.v = 0


class Buf:

    def __init__(self, tiles):
        self.tiles = tiles
        self.i = 0

    def acquire(self):
        t = self.tiles[self.i % len(self.tiles)]
        self.i += 1
        return t


def build(debug=False):
    nc = bass.Bass("TRN2", target_bir_lowering=False)

    x0c = nc.dram_tensor("x0c", [NSC, TSC, DM], F32, kind="ExternalInput")
    in_wT = nc.dram_tensor("in_wT", [DM, 2 * DI], F32, kind="ExternalInput")
    ln_g = nc.dram_tensor("ln_g", [DM], F32, kind="ExternalInput")
    ln_b = nc.dram_tensor("ln_b", [DM], F32, kind="ExternalInput")
    convw_sl = nc.dram_tensor("convw_sl", [DSL, 4], F32, kind="ExternalInput")
    convb_sl = nc.dram_tensor("convb_sl", [DSL], F32, kind="ExternalInput")
    xprojT_sl = nc.dram_tensor("xprojT_sl", [DSL, 96], F32, kind="ExternalInput")
    dt_wT_sl = nc.dram_tensor("dt_wT_sl", [DTR, DSL], F32, kind="ExternalInput")
    dt_b_sl = nc.dram_tensor("dt_b_sl", [DSL], F32, kind="ExternalInput")
    A_log_sl = nc.dram_tensor("A_log_sl", [DSL, NST], F32, kind="ExternalInput")
    D_sl = nc.dram_tensor("D_sl", [DSL], F32, kind="ExternalInput")
    out_wT_sl = nc.dram_tensor("out_wT_sl", [DSL, DM], F32, kind="ExternalInput")
    centT = nc.dram_tensor("centT", [DM, NEXP], F32, kind="ExternalInput")
    gam_e = nc.dram_tensor("gam_e", [DM], F32, kind="ExternalInput")
    bet_e = nc.dram_tensor("bet_e", [DM], F32, kind="ExternalInput")
    b1_e = nc.dram_tensor("b1_e", [FF], F32, kind="ExternalInput")
    b2_e = nc.dram_tensor("b2_e", [DM], F32, kind="ExternalInput")
    W1T_e = nc.dram_tensor("W1T_e", [DM, FF], F32, kind="ExternalInput")
    W2T_e = nc.dram_tensor("W2T_e", [FF, DM], F32, kind="ExternalInput")
    bidx = nc.dram_tensor("bidx", [32, 128], U32, kind="ExternalInput")
    fidx = nc.dram_tensor("fidx", [32, 128], U32, kind="ExternalInput")

    x2o = nc.dram_tensor("x2o", [NSC, TSC, DM], F32, kind="ExternalOutput")

    dbg_pairs = []

    def scratch(name, shape, dt, dbg=False, space="Local"):
        t = nc.dram_tensor(name, shape, dt, kind="Internal", addr_space=space)
        if debug and dbg:
            o = nc.dram_tensor("dbg_" + name, shape, dt, kind="ExternalOutput")
            dbg_pairs.append((o, t))
        return t

    u_d = scratch("u_d", [NSC, TSC, DM], F32, dbg=True)
    e1_in = [scratch(f"e1_in{p}", [2 * DI, TSC], BF16, dbg=True) for p in range(NSC)]
    e1_out = [scratch(f"e1_out{p}", [4 * 2 * DI, TSC], BF16) for p in range(NSC)]
    dbl_in = [scratch(f"dbl_in{p}", [4, 96, TSC], F32) for p in range(NSC)]
    dbl_out = [scratch(f"dbl_out{p}", [4, 96, TSC], F32, dbg=True) for p in range(NSC)]
    rs_in = [scratch(f"rs_in{p}", [GT, DM], BF16) for p in range(NSC)]
    rs_out = [scratch(f"rs_out{p}", [TSC, DM], BF16, dbg=True) for p in range(NSC)]
    nh_in = [scratch(f"nh_in{p}", [DM, TSC], BF16) for p in range(NSC)]
    nh_out = [scratch(f"nh_out{p}", [8 * DM, TSC], BF16, dbg=True, space="Shared")
              for p in range(NSC)]
    f_in = [scratch(f"f_in{p}", [GA, DM], BF16, dbg=True) for p in range(NSC)]
    f_out = [scratch(f"f_out{p}", [8 * GA, DM], BF16, dbg=True, space="Shared")
             for p in range(NSC)]
    w1bf_d = scratch("w1bf_d", [8, 8, 128, 512], BF16)
    w2bf_d = scratch("w2bf_d", [FF, DM], BF16)
    x1_d = scratch("x1_d", [NSC, TSC, DM], F32, dbg=True)
    w_d = scratch("w_d", [NSC, TSC, DM], F32, dbg=True)

    ops = {e: [] for e in ENGS}
    dbg_sbuf = []
    dbg_sbuf_outs = {}

    with ExitStack() as ctx:
        E = ctx.enter_context

        sems = {n: SemC(E(nc.semaphore(n)))
                for n in ("sS", "sV", "sT", "sG")}
        sS, sV, sT, sG = sems["sS"], sems["sV"], sems["sT"], sems["sG"]
        esem = {"scalar": sS, "vector": sV, "tensor": sT, "gpsimd": sG, "sync": None}

        class QPool:

            def __init__(self, name, k):
                self.sems = [SemC(E(nc.semaphore(f"{name}{i}"))) for i in range(k)]
                self.i = 0

            def next(self):
                s = self.sems[self.i % len(self.sems)]
                self.i += 1
                prev = (s, s.v) if s.v else None
                return s, prev

        qs_p = QPool("qs", 12)
        qg_p = QPool("qg", 12)
        qo_p = QPool("qo", 6)
        cc_p = QPool("cc", 6)

        trk_w = {}
        trk_r = {}

        def O(eng, fn, r=(), w=(), waits=(), inc=None, inc_n=1):
            deps = {}

            def add(m):
                if m is None:
                    return
                s, v = m
                k = id(s)
                if k not in deps or deps[k][1] < v:
                    deps[k] = (s, v)

            for m in waits:
                add(m)
            rn = [ap.tensor.name for ap in r]
            wn = [ap.tensor.name for ap in w]
            for n in rn:
                add(trk_w.get(n))
            for n in wn:
                add(trk_w.get(n))
                for m in trk_r.get(n, {}).values():
                    add(m)
            ws = []
            for (s, v) in deps.values():
                if eng == "tensor" and s is sT:
                    continue
                ws.append((s, v))
            if inc is not None:
                inc.v += inc_n
                mark = (inc, inc.v)
            else:
                mark = None
            ops[eng].append((fn, ws, inc, inc_n if inc else 0))
            if mark is not None:
                for n in rn:
                    d = trk_r.setdefault(n, {})
                    k = id(mark[0])
                    if k not in d or d[k][1] < mark[1]:
                        d[k] = mark
                for n in wn:
                    trk_w[n] = mark
                    trk_r[n] = {}
            return mark

        def C(eng, fn, r=(), w=(), waits=()):
            return O(eng, fn, r, w, waits, inc=esem[eng], inc_n=1)

        def DMA(queue, out_ap, in_ap, waits=(), **kw):
            if queue == "sync":
                pool, eng = qs_p, "sync"
            elif queue == "out":
                pool, eng = qo_p, "gpsimd"
            else:
                pool, eng = qg_p, "gpsimd"
            s, prev = pool.next()

            def f(e, o=out_ap, i=in_ap, kw=kw):
                return e.dma_start(o, i, **kw)
            return O(eng, f, r=[in_ap], w=[out_ap],
                     waits=list(waits) + [prev], inc=s, inc_n=16)

        def GATHER(out_ap, in_ap, idx_ap, waits=()):
            s, prev = qg_p.next()

            def f(e, o=out_ap, i=in_ap, x=idx_ap):
                return e.indirect_dma_start(
                    out=o, out_offset=None, in_=i,
                    in_offset=bass.IndirectOffsetOnAxis(ap=x, axis=0))
            return O("gpsimd", f, r=[in_ap, idx_ap], w=[out_ap],
                     waits=list(waits) + [prev], inc=s, inc_n=16)

        def COLL(kind, op, groups, in_ap, out_ap, waits=()):
            s, prev = cc_p.next()

            def f(e, k=kind, o=op, g=groups, i=in_ap, u=out_ap):
                return e.collective_compute(k, o, replica_groups=g, ins=[i], outs=[u])
            return O("gpsimd", f, r=[in_ap], w=[out_ap],
                     waits=list(waits) + [prev], inc=s, inc_n=1)

        def sb(name, p, f, dt):
            return E(nc.sbuf_tensor(name, [p, f], dt))

        def ring(name, n, p, f, dt):
            return Buf([sb(f"{name}{i}", p, f, dt) for i in range(n)])

        idn_bf = sb("idn_bf", 128, 128, BF16)
        idn_f = sb("idn_f", 128, 128, F32)
        iota_f = sb("iota_f", 128, 128, F32)
        xpsl_sb = sb("xpsl_sb", 128, 4 * 96, BF16)
        dtwsl_sb = sb("dtwsl_sb", 64, DSL, BF16)
        A_sb = sb("A_sb", 128, 64, F32)
        convw_sb = sb("convw_sb", 128, 16, F32)
        convb_sb = sb("convb_sb", 128, 4, F32)
        dtb_sb = sb("dtb_sb", 128, 4, F32)
        Dv_sb = sb("Dv_sb", 128, 4, F32)
        lng_sb = sb("lng_sb", 128, 8, F32)
        lnb_sb = sb("lnb_sb", 128, 8, F32)
        gam_sb = sb("gam_sb", 128, 8, F32)
        bet_sb = sb("bet_sb", 128, 8, F32)
        betbf_sb = sb("betbf_sb", 128, 8, BF16)
        b1_sb = sb("b1_sb", 128, 32, F32)
        bias1_sb = sb("bias1_sb", 128, 32, F32)
        b2r_sb = sb("b2r_sb", 1, DM, BF16)
        ones1_sb = sb("ones1_sb", 1, 128, BF16)
        ones8_sb = sb("ones8_sb", 128, 8, F32)
        centT_sb = sb("centT_sb", 128, 64, F32)
        state_sb = sb("state_sb", 128, 64, F32)
        bidx_sb = sb("bidx_sb", 128, 32, U32)
        fidx_sb = sb("fidx_sb", 128, 32, U32)
        diag_sb = sb("diag_sb", 128, 16 * 128, BF16)
        outw_sb = sb("outw_sb", 128, 4 * DM, BF16)
        epsone_sb = sb("epsone_sb", 128, 2, F32)
        trash = sb("trash", 128, DM, BF16)

        x0t_b = ring("x0t", 2, 128, DM, F32)
        colb = ring("colb", 3, 128, 8, F32)
        u_b = ring("u_b", 2, 128, DM, F32)
        lnu_b = ring("lnu", 1, 128, DM, BF16)
        LNuT = sb("LNuT", 128, 8 * TSC, BF16)
        inwj_b = ring("inwj", 2, 128, DM, BF16)
        xzev_b = ring("xzev", 2, 128, TSC, BF16)

        convin = [sb(f"convin{i}", 128, 3 + TSC, BF16) for i in range(4)]
        xi2blk = [sb(f"xi2blk{i}", 128, TSC, BF16) for i in range(4)]
        dtT_b = ring("dtTb", 2, 64, TSC, BF16)
        sp_b = ring("spb", 2, 128, TSC, F32)
        dlblk = [sb(f"dlblk{i}", 128, TSC, BF16) for i in range(4)]
        dublk = [sb(f"dublk{i}", 128, TSC, BF16) for i in range(4)]
        brep_b = ring("brep", 2, 128, TSC, BF16)
        crep_b = ring("crep", 2, 128, TSC, BF16)
        dA_b = ring("dAb", 2, 128, TSC, F32)
        X_b = ring("Xb", 2, 128, TSC, BF16)
        H_b = ring("Hb", 2, 128, TSC, BF16)
        P_b = ring("Pb", 2, 128, TSC, BF16)
        yacc = [sb(f"yacc{i}", 128, TSC, BF16) for i in range(4)]
        zblk_b = ring("zblk", 1, 128, TSC, BF16)
        szb_b = ring("szb", 1, 128, TSC, BF16)
        y2blk = [sb(f"y2blk{i}", 128, TSC, BF16) for i in range(4)]
        rsev_b = ring("rsev", 2, 128, TSC, BF16)

        m_b = ring("m_b", 1, 128, DM, BF16)
        aC_b = ring("aC", 2, 128, DM, F32)
        x0m_b = ring("x0m", 1, 128, DM, F32)
        nhat_b = ring("nhat", 1, 128, DM, BF16)
        wT_b = ring("wTb", 4, 128, 128, F32)
        nhT_b = ring("nhT", 2, 128, 128, BF16)
        scm_b = ring("scm", 4, 128, 8, F32)
        sct_b = ring("sct", 2, 128, 16, F32)
        msk_sb = sb("msk_sb", 128, 4 * 4 * 8, F32)
        alf_sb = sb("alf_sb", 128, 32, F32)

        w1c = [sb(f"w1c{i}", 128, 512, BF16) for i in range(8)]
        nhD = [sb(f"nhD{i}", 128, 512, BF16) for i in range(8)]
        w2h = [sb(f"w2h{i}", 128, 512, BF16) for i in range(16)]
        r1sb = [sb(f"r1sb{i}", 128, 512, BF16) for i in range(32)]
        o2_b = ring("o2b", 2, 128, 512, BF16)

        sl_b = ring("slb", 2, 128, DM, BF16)
        acc_b = ring("accb", 1, 128, DM, F32)
        wE_b = ring("wEb", 1, 128, DM, F32)
        x1E_b = ring("x1E", 1, 128, DM, F32)

        if debug:
            for nm, ap, shp, dt in [
                ("bias1", bias1_sb[:, :], [128, 32], F32),
                ("msk", msk_sb[:, :], [128, 128], F32),
                ("alf", alf_sb[:, :], [128, 32], F32),
                ("A", A_sb[:, :], [128, 64], F32),
                ("state", state_sb[:, :], [128, 64], F32),
            ]:
                dbg_sbuf.append((nm, ap))
                dbg_sbuf_outs[nm] = nc.dram_tensor("dbgsb_" + nm, shp, dt,
                                                   kind="ExternalOutput")

        PM = [E(nc.psum_tensor(f"pm{i}", [128, 512], F32)) for i in range(6)]
        pm_b = Buf(PM)
        PX = E(nc.psum_tensor("px", [96, 512], F32))
        PAUX = E(nc.psum_tensor("paux", [128, 512], F32))
        PTF = PAUX[:, 0:128]
        PT = PAUX[:, 128:192].bitcast(BF16)
        PC1 = PAUX[:, 192:193]

        def col(t, i):
            return t[:, i:i + 1]

        def ln_pipeline(x_ap, fdim):
            ct = colb.acquire()
            mean_c, m2_c, var_c, r_c, mb_c = (col(ct, k) for k in range(5))
            inv = 1.0 / fdim
            sq = float(1.0 / np.sqrt(fdim))
            C("scalar", lambda e: e.activation(
                trash[:, :fdim], x_ap, AF.Copy, scale=inv, accum_out=mean_c),
                r=[x_ap], w=[trash[:, :fdim], mean_c])
            C("scalar", lambda e: e.activation(
                trash[:, :fdim], x_ap, AF.Square, scale=sq, accum_out=m2_c),
                r=[x_ap], w=[trash[:, :fdim], m2_c])
            C("vector", lambda e: e.tensor_scalar(
                var_c, mean_c, mean_c, -1.0, op0=OP.mult, op1=OP.mult),
                r=[mean_c], w=[var_c])
            C("vector", lambda e: e.tensor_tensor(var_c, var_c, m2_c, op=OP.add),
                r=[var_c, m2_c], w=[var_c])
            C("scalar", lambda e: e.activation(var_c, var_c, AF.Ln, bias=EPS),
                r=[var_c, epsone_sb[:, 0:1]], w=[var_c])
            C("scalar", lambda e: e.activation(r_c, var_c, AF.Exp, scale=-0.5),
                r=[var_c], w=[r_c])
            C("vector", lambda e: e.tensor_scalar(
                mb_c, mean_c, r_c, -1.0, op0=OP.mult, op1=OP.mult),
                r=[mean_c, r_c], w=[mb_c])
            return r_c, mb_c

        def startup():
            C("vector", lambda e: e.memset(epsone_sb[:, 0:1], EPS),
              w=[epsone_sb[:, 0:1]])
            nc.const_aps.register(F32, EPS, epsone_sb[:, 0:1])
            C("vector", lambda e: e.memset(epsone_sb[:, 1:2], 1.0),
              w=[epsone_sb[:, 1:2]])
            nc.const_aps.register(F32, 1.0, epsone_sb[:, 1:2])
            C("gpsimd", lambda e: e.iota(iota_f[:, :], [[1, 128]], base=0,
                                         channel_multiplier=-1,
                                         allow_small_or_imprecise_dtypes=True),
              w=[iota_f[:, :]])
            C("vector", lambda e: e.tensor_scalar(
                idn_f[:, :], iota_f[:, :], 0.0, None, op0=OP.is_equal),
                r=[iota_f[:, :]], w=[idn_f[:, :]])
            C("vector", lambda e: e.tensor_copy(idn_bf[:, :], idn_f[:, :]),
                r=[idn_f[:, :]], w=[idn_bf[:, :]])
            C("vector", lambda e: e.memset(ones8_sb[:, :], 1.0), w=[ones8_sb[:, :]])
            C("vector", lambda e: e.memset(state_sb[:, :], 0.0), w=[state_sb[:, :]])
            C("gpsimd", lambda e: e.memset(ones1_sb[:, :], 1.0), w=[ones1_sb[:, :]])
            for dt in range(4):
                C("vector", lambda e, dt=dt: e.memset(convin[dt][:, 0:3], 0.0),
                  w=[convin[dt][:, 0:3]])

            DMA("gpsimd", xpsl_sb[:, :],
                bass.AP(xprojT_sl, 0, [[96, 128], [96 * 128, 4], [1, 96]]))
            DMA("gpsimd", dtwsl_sb[:, :], dt_wT_sl[:, :])
            DMA("sync", A_sb[:, :],
                bass.AP(A_log_sl, 0, [[16, 128], [16 * 128, 4], [1, 16]]))
            DMA("sync", convw_sb[:, :],
                bass.AP(convw_sl, 0, [[4, 128], [4 * 128, 4], [1, 4]]))
            slow = dict(allow_slow_non_contiguous=True)
            DMA("sync", convb_sb[:, :], bass.AP(convb_sl, 0, [[1, 128], [128, 4]]), **slow)
            DMA("sync", dtb_sb[:, :], bass.AP(dt_b_sl, 0, [[1, 128], [128, 4]]), **slow)
            DMA("sync", Dv_sb[:, :], bass.AP(D_sl, 0, [[1, 128], [128, 4]]), **slow)
            DMA("sync", lng_sb[:, :], bass.AP(ln_g, 0, [[1, 128], [128, 8]]), **slow)
            DMA("sync", lnb_sb[:, :], bass.AP(ln_b, 0, [[1, 128], [128, 8]]), **slow)
            DMA("sync", gam_sb[:, :], bass.AP(gam_e, 0, [[1, 128], [128, 8]]), **slow)
            DMA("sync", bet_sb[:, :], bass.AP(bet_e, 0, [[1, 128], [128, 8]]), **slow)
            C("vector", lambda e: e.tensor_copy(betbf_sb[:, :], bet_sb[:, :]),
              r=[bet_sb[:, :]], w=[betbf_sb[:, :]])
            DMA("sync", b1_sb[:, :], bass.AP(b1_e, 0, [[1, 128], [128, 32]]), **slow)
            DMA("gpsimd", b2r_sb[:, :], b2_e[None, :])
            DMA("sync", centT_sb[:, :],
                bass.AP(centT, 0, [[8, 128], [8 * 128, 8], [1, 8]]))
            DMA("sync", bidx_sb[:, :], bass.AP(bidx, 0, [[1, 128], [128, 32]]), **slow)
            DMA("sync", fidx_sb[:, :], bass.AP(fidx, 0, [[1, 128], [128, 32]]), **slow)

            C("scalar", lambda e: e.activation(A_sb[:, :], A_sb[:, :], AF.Exp),
              r=[A_sb[:, :]], w=[A_sb[:, :]])
            C("vector", lambda e: e.tensor_scalar(
                A_sb[:, :], A_sb[:, :], -1.0, None, op0=OP.mult),
                r=[A_sb[:, :]], w=[A_sb[:, :]])

            for c0 in range(16):
                C("vector", lambda e, c0=c0: e.tensor_scalar(
                    diag_sb[:, c0 * 128:(c0 + 1) * 128], idn_f[:, :],
                    col(convw_sb, c0), None, op0=OP.mult),
                    r=[idn_f[:, :], col(convw_sb, c0)],
                    w=[diag_sb[:, c0 * 128:(c0 + 1) * 128]])

            for dt in range(4):
                DMA("gpsimd", outw_sb[:, dt * DM:(dt + 1) * DM],
                    out_wT_sl[dt * 128:(dt + 1) * 128, :])

            for ht in range(32):
                t = r1sb[ht]
                DMA("gpsimd", t[:, 0:512], W2T_e[ht * 128:(ht + 1) * 128, 0:512])
                DMA("gpsimd", w2bf_d[ht * 128:(ht + 1) * 128, 0:512], t[:, 0:512])
                t2 = w2h[ht % 16]
                DMA("gpsimd", t2[:, 0:512], W2T_e[ht * 128:(ht + 1) * 128, 512:DM])
                DMA("gpsimd", w2bf_d[ht * 128:(ht + 1) * 128, 512:DM], t2[:, 0:512])
            for hc in range(8):
                for kt in range(8):
                    t = r1sb[(hc * 8 + kt) % 32]
                    DMA("gpsimd", t[:, 0:512],
                        bass.AP(W1T_e, kt * 128 * FF + hc * 512,
                                [[FF, 128], [1, 512]]))
                    DMA("gpsimd", w1bf_d[hc, kt, :, :], t[:, 0:512])

        def stage_A(sc):
            par = sc
            for ti in range(4):
                xt = x0t_b.acquire()
                DMA("sync", xt[:, :], x0c[sc, ti * 128:(ti + 1) * 128, :])
                r_c, mb_c = ln_pipeline(xt[:, :], DM)
                ut = u_b.acquire()
                C("vector", lambda e, xt=xt, ut=ut, r=r_c, mb=mb_c: e.tensor_scalar(
                    ut[:, :], xt[:, :], r, mb, op0=OP.mult, op1=OP.add),
                    r=[xt[:, :], r_c, mb_c], w=[ut[:, :]])
                DMA("out", u_d[sc, ti * 128:(ti + 1) * 128, :], ut[:, :])
                lt = lnu_b.acquire()
                C("vector", lambda e, ut=ut, lt=lt: e.tensor_copy(lt[:, :], ut[:, :]),
                    r=[ut[:, :]], w=[lt[:, :]])
                for kt in range(8):
                    C("tensor", lambda e, lt=lt, kt=kt: e.transpose(
                        PT, lt[:, kt * 128:(kt + 1) * 128], idn_bf[:, :]),
                        r=[lt[:, kt * 128:(kt + 1) * 128], idn_bf[:, :]], w=[PT])
                    C("vector", lambda e, kt=kt, ti=ti: e.tensor_copy(
                        LNuT[:, kt * TSC + ti * 128: kt * TSC + (ti + 1) * 128], PT),
                        r=[PT],
                        w=[LNuT[:, kt * TSC + ti * 128: kt * TSC + (ti + 1) * 128]])

            for kt in range(8):
                C("vector", lambda e, kt=kt: e.tensor_scalar(
                    LNuT[:, kt * TSC:(kt + 1) * TSC], LNuT[:, kt * TSC:(kt + 1) * TSC],
                    col(lng_sb, kt), col(lnb_sb, kt), op0=OP.mult, op1=OP.add),
                    r=[LNuT[:, kt * TSC:(kt + 1) * TSC], col(lng_sb, kt),
                       col(lnb_sb, kt)],
                    w=[LNuT[:, kt * TSC:(kt + 1) * TSC]])

            for j in range(32):
                wt = inwj_b.acquire()
                DMA("gpsimd", wt[:, :],
                    bass.AP(in_wT, j * 128,
                            [[2 * DI, 128], [128 * 2 * DI, 8], [1, 128]]))
                PA = pm_b.acquire()
                for kt in range(8):
                    C("tensor", lambda e, kt=kt, wt=wt, PA=PA: e.matmul(
                        PA[:, :], wt[:, kt * 128:(kt + 1) * 128],
                        LNuT[:, kt * TSC:(kt + 1) * TSC],
                        start=(kt == 0), stop=(kt == 7)),
                        r=[wt[:, kt * 128:(kt + 1) * 128],
                           LNuT[:, kt * TSC:(kt + 1) * TSC]],
                        w=[PA[:, :]])
                zt = xzev_b.acquire()
                C("scalar", lambda e, zt=zt, PA=PA: e.activation(
                    zt[:, :], PA[:, :], AF.Copy), r=[PA[:, :]], w=[zt[:, :]])
                DMA("gpsimd", e1_in[par][j * 128:(j + 1) * 128, :], zt[:, :])

            COLL("AllGather", OP.bypass, GROUP4,
                 e1_in[par].ap(), e1_out[par].ap())

        def stage_B(sc):
            par = sc
            for tch in range(4):
                for dt in range(4):
                    GATHER(convin[dt][:, 3:3 + TSC], e1_out[par][:, :],
                           col(bidx_sb, tch * 8 + dt))
                    PC = pm_b.acquire()
                    for tap in range(4):
                        C("tensor", lambda e, dt=dt, tap=tap, PC=PC: e.matmul(
                            PC[:, :],
                            diag_sb[:, (dt * 4 + tap) * 128:(dt * 4 + tap + 1) * 128],
                            convin[dt][:, tap:tap + TSC],
                            start=(tap == 0), stop=(tap == 3)),
                            r=[diag_sb[:, (dt * 4 + tap) * 128:
                                       (dt * 4 + tap + 1) * 128],
                               convin[dt][:, :]],
                            w=[PC[:, :]])
                    C("vector", lambda e, dt=dt: e.tensor_copy(
                        convin[dt][:, 0:3], convin[dt][:, TSC:TSC + 3]),
                        r=[convin[dt][:, :]], w=[convin[dt][:, 0:3]])
                    C("scalar", lambda e, dt=dt, PC=PC: e.activation(
                        xi2blk[dt][:, :], PC[:, :], AF.Silu, bias=col(convb_sb, dt)),
                        r=[PC[:, :], col(convb_sb, dt)], w=[xi2blk[dt][:, :]])
                    C("tensor", lambda e, dt=dt: e.matmul(
                        PX[:, :], xpsl_sb[:, dt * 96:(dt + 1) * 96], xi2blk[dt][:, :],
                        start=(dt == 0), stop=(dt == 3)),
                        r=[xpsl_sb[:, dt * 96:(dt + 1) * 96], xi2blk[dt][:, :]],
                        w=[PX[:, :]])
                st_ = sp_b.acquire()
                C("scalar", lambda e, st_=st_: e.activation(
                    st_[0:96, 0:TSC], PX[:, :], AF.Copy),
                    r=[PX[:, :]], w=[st_[:, :]])
                DMA("gpsimd", dbl_in[par][tch, :, :], st_[0:96, 0:TSC])
                COLL("AllReduce", OP.add, GROUP4,
                     dbl_in[par][tch, :, :], dbl_out[par][tch, :, :])

                dtt = dtT_b.acquire()
                DMA("gpsimd", dtt[:, :], dbl_out[par][tch, 0:64, :])
                for mt in range(4):
                    PB = pm_b.acquire()
                    C("tensor", lambda e, mt=mt, PB=PB, dtt=dtt: e.matmul(
                        PB[:, :], dtwsl_sb[:, mt * 128:(mt + 1) * 128], dtt[:, :],
                        start=True, stop=True),
                        r=[dtwsl_sb[:, mt * 128:(mt + 1) * 128], dtt[:, :]],
                        w=[PB[:, :]])
                    s1 = sp_b.acquire()
                    s2 = sp_b.acquire()
                    C("scalar", lambda e, s1=s1, PB=PB, mt=mt: e.activation(
                        s1[:, :], PB[:, :], AF.Abs, bias=col(dtb_sb, mt)),
                        r=[PB[:, :], col(dtb_sb, mt)], w=[s1[:, :]])
                    C("scalar", lambda e, s1=s1: e.activation(
                        s1[:, :], s1[:, :], AF.Exp, scale=-1.0),
                        r=[s1[:, :]], w=[s1[:, :]])
                    C("scalar", lambda e, s1=s1: e.activation(
                        s1[:, :], s1[:, :], AF.Ln, bias=1.0),
                        r=[s1[:, :], epsone_sb[:, 1:2]], w=[s1[:, :]])
                    C("scalar", lambda e, s2=s2, PB=PB, mt=mt: e.activation(
                        s2[:, :], PB[:, :], AF.Relu, bias=col(dtb_sb, mt)),
                        r=[PB[:, :], col(dtb_sb, mt)], w=[s2[:, :]])
                    C("vector", lambda e, mt=mt, s1=s1, s2=s2: e.tensor_tensor(
                        dlblk[mt][:, :], s1[:, :], s2[:, :], op=OP.add),
                        r=[s1[:, :], s2[:, :]], w=[dlblk[mt][:, :]])
                    C("vector", lambda e, mt=mt: e.tensor_tensor(
                        dublk[mt][:, :], dlblk[mt][:, :], xi2blk[mt][:, :],
                        op=OP.mult),
                        r=[dlblk[mt][:, :], xi2blk[mt][:, :]], w=[dublk[mt][:, :]])

                for n in range(NST):
                    bt = brep_b.acquire()
                    DMA("gpsimd", bt[:, :],
                        bass.AP(dbl_out[par], tch * 96 * TSC + (64 + n) * TSC,
                                [[0, 128], [1, TSC]]))
                    ct2 = crep_b.acquire()
                    DMA("gpsimd", ct2[:, :],
                        bass.AP(dbl_out[par], tch * 96 * TSC + (80 + n) * TSC,
                                [[0, 128], [1, TSC]]))
                    for dt in range(4):
                        at = dA_b.acquire()
                        C("scalar", lambda e, at=at, dt=dt, n=n: e.activation(
                            at[:, :], dlblk[dt][:, :], AF.Exp,
                            scale=col(A_sb, dt * 16 + n)),
                            r=[dlblk[dt][:, :], col(A_sb, dt * 16 + n)],
                            w=[at[:, :]])
                        xt2 = X_b.acquire()
                        C("vector", lambda e, xt2=xt2, dt=dt, bt=bt: e.tensor_tensor(
                            xt2[:, :], dublk[dt][:, :], bt[:, :], op=OP.mult),
                            r=[dublk[dt][:, :], bt[:, :]], w=[xt2[:, :]])
                        ht_ = H_b.acquire()
                        scol = col(state_sb, n * 4 + dt)
                        C("vector", lambda e, ht_=ht_, at=at, xt2=xt2, s=scol:
                          e.tensor_tensor_scan(ht_[:, :], at[:, :], xt2[:, :], s,
                                               op0=OP.mult, op1=OP.add),
                          r=[at[:, :], xt2[:, :], scol], w=[ht_[:, :]])
                        C("vector", lambda e, ht_=ht_, s=scol: e.tensor_copy(
                            s, ht_[:, TSC - 1:TSC]),
                            r=[ht_[:, :]], w=[scol])
                        if n == 0:
                            C("vector", lambda e, dt=dt, ht_=ht_, ct2=ct2:
                              e.tensor_tensor(yacc[dt][:, :], ht_[:, :],
                                              ct2[:, :], op=OP.mult),
                              r=[ht_[:, :], ct2[:, :]], w=[yacc[dt][:, :]])
                        else:
                            pt3 = P_b.acquire()
                            C("vector", lambda e, pt3=pt3, ht_=ht_, ct2=ct2:
                              e.tensor_tensor(pt3[:, :], ht_[:, :], ct2[:, :],
                                              op=OP.mult),
                              r=[ht_[:, :], ct2[:, :]], w=[pt3[:, :]])
                            C("vector", lambda e, dt=dt, pt3=pt3: e.tensor_tensor(
                                yacc[dt][:, :], yacc[dt][:, :], pt3[:, :],
                                op=OP.add),
                                r=[yacc[dt][:, :], pt3[:, :]], w=[yacc[dt][:, :]])

                for dt in range(4):
                    zt = zblk_b.acquire()
                    GATHER(zt[:, :], e1_out[par][:, :],
                           col(bidx_sb, tch * 8 + 4 + dt))
                    szt = szb_b.acquire()
                    C("scalar", lambda e, szt=szt, zt=zt: e.activation(
                        szt[:, :], zt[:, :], AF.Silu),
                        r=[zt[:, :]], w=[szt[:, :]])
                    C("vector", lambda e, dt=dt: e.tensor_scalar(
                        y2blk[dt][:, :], xi2blk[dt][:, :], col(Dv_sb, dt), None,
                        op0=OP.mult),
                        r=[xi2blk[dt][:, :], col(Dv_sb, dt)], w=[y2blk[dt][:, :]])
                    C("vector", lambda e, dt=dt: e.tensor_tensor(
                        y2blk[dt][:, :], y2blk[dt][:, :], yacc[dt][:, :], op=OP.add),
                        r=[y2blk[dt][:, :], yacc[dt][:, :]], w=[y2blk[dt][:, :]])
                    C("vector", lambda e, dt=dt, szt=szt: e.tensor_tensor(
                        y2blk[dt][:, :], y2blk[dt][:, :], szt[:, :], op=OP.mult),
                        r=[y2blk[dt][:, :], szt[:, :]], w=[y2blk[dt][:, :]])

                for tt in range(4):
                    for dh in range(2):
                        PO = pm_b.acquire()
                        for dt in range(4):
                            C("tensor", lambda e, dt=dt, tt=tt, dh=dh, PO=PO:
                              e.matmul(
                                  PO[:, :], y2blk[dt][:, tt * 128:(tt + 1) * 128],
                                  outw_sb[:, dt * DM + dh * 512:
                                          dt * DM + (dh + 1) * 512],
                                  start=(dt == 0), stop=(dt == 3)),
                              r=[y2blk[dt][:, :],
                                 outw_sb[:, dt * DM + dh * 512:
                                         dt * DM + (dh + 1) * 512]],
                              w=[PO[:, :]])
                        rt = rsev_b.acquire()
                        C("scalar", lambda e, rt=rt, PO=PO: e.activation(
                            rt[:, :], PO[:, :], AF.Copy),
                            r=[PO[:, :]], w=[rt[:, :]])
                        DMA("gpsimd",
                            rs_in[par][tch * 512 + tt * 128:
                                       tch * 512 + (tt + 1) * 128,
                                       dh * 512:(dh + 1) * 512],
                            rt[:, :])

            COLL("ReduceScatter", OP.add, GROUP4,
                 rs_in[par].ap(), rs_out[par].ap())

        msk_done = [None] * NSC

        def stage_C(sc):
            par = sc
            sc_tiles = []
            for tt in range(4):
                mt_ = m_b.acquire()
                DMA("sync", mt_[:, :], rs_out[par][tt * 128:(tt + 1) * 128, :])
                at = aC_b.acquire()
                DMA("sync", at[:, :], u_d[sc, tt * 128:(tt + 1) * 128, :])
                C("vector", lambda e, at=at, mt_=mt_: e.scalar_tensor_tensor(
                    at[:, :], mt_[:, :], 0.1, at[:, :], op0=OP.mult, op1=OP.add),
                    r=[mt_[:, :], at[:, :]], w=[at[:, :]])
                r_c, mb_c = ln_pipeline(at[:, :], DM)
                C("vector", lambda e, at=at, r=r_c, mb=mb_c: e.tensor_scalar(
                    at[:, :], at[:, :], r, mb, op0=OP.mult, op1=OP.add),
                    r=[at[:, :], r_c, mb_c], w=[at[:, :]])
                xt = x0m_b.acquire()
                DMA("sync", xt[:, :], x0c[sc, tt * 128:(tt + 1) * 128, :])
                C("vector", lambda e, xt=xt, at=at: e.tensor_tensor(
                    xt[:, :], xt[:, :], at[:, :], op=OP.add),
                    r=[xt[:, :], at[:, :]], w=[xt[:, :]])
                DMA("out", x1_d[sc, tt * 128:(tt + 1) * 128, :], xt[:, :])
                r2, mb2 = ln_pipeline(xt[:, :], DM)
                C("vector", lambda e, at=at, xt=xt, r=r2, mb=mb2: e.tensor_scalar(
                    at[:, :], xt[:, :], r, mb, op0=OP.mult, op1=OP.add),
                    r=[xt[:, :], r2, mb2], w=[at[:, :]])
                DMA("out", w_d[par, tt * 128:(tt + 1) * 128, :], at[:, :])
                r3, mb3 = ln_pipeline(at[:, :], DM)
                nt_ = nhat_b.acquire()
                C("vector", lambda e, nt_=nt_, at=at, r=r3, mb=mb3: e.tensor_scalar(
                    nt_[:, :], at[:, :], r, mb, op0=OP.mult, op1=OP.add),
                    r=[at[:, :], r3, mb3], w=[nt_[:, :]])
                for kt in range(8):
                    C("tensor", lambda e, nt_=nt_, kt=kt: e.transpose(
                        PT, nt_[:, kt * 128:(kt + 1) * 128], idn_bf[:, :]),
                        r=[nt_[:, kt * 128:(kt + 1) * 128], idn_bf[:, :]], w=[PT])
                    nt2 = nhT_b.acquire()
                    C("vector", lambda e, nt2=nt2: e.tensor_copy(nt2[:, :], PT),
                        r=[PT], w=[nt2[:, :]])
                    DMA("gpsimd",
                        nh_in[par][kt * 128:(kt + 1) * 128,
                                   tt * 128:(tt + 1) * 128],
                        nt2[:, :])
                PS = pm_b.acquire()
                for kt in range(8):
                    C("tensor", lambda e, at=at, kt=kt: e.transpose(
                        PTF, at[:, kt * 128:(kt + 1) * 128], idn_f[:, :]),
                        r=[at[:, kt * 128:(kt + 1) * 128], idn_f[:, :]], w=[PTF])
                    wt2 = wT_b.acquire()
                    C("vector", lambda e, wt2=wt2: e.tensor_copy(wt2[:, :], PTF),
                        r=[PTF], w=[wt2[:, :]])
                    C("tensor", lambda e, kt=kt, wt2=wt2, PS=PS: e.matmul(
                        PS[:, 0:8], wt2[:, :], centT_sb[:, kt * 8:(kt + 1) * 8],
                        start=(kt == 0), stop=(kt == 7)),
                        r=[wt2[:, :], centT_sb[:, kt * 8:(kt + 1) * 8]],
                        w=[PS[:, :]])
                st0 = scm_b.acquire()
                C("vector", lambda e, st0=st0, PS=PS: e.tensor_copy(
                    st0[:, :], PS[:, 0:8]), r=[PS[:, :]], w=[st0[:, :]])
                sc_tiles.append(st0)

            COLL("AllGather", OP.bypass, GROUP8,
                 nh_in[par].ap(), nh_out[par].ap())

            mmarks = []
            for tt in range(4):
                st_ = sc_tiles[tt]
                tts = sct_b.acquire()
                C("vector", lambda e, tts=tts, st_=st_: e.tensor_reduce(
                    col(tts, 0), st_[:, :], axis=AX.X, op=OP.max),
                    r=[st_[:, :]], w=[col(tts, 0)])
                mb_ = (par * 4 + tt) * 8
                mcols = msk_sb[:, mb_:mb_ + 8]
                C("vector", lambda e, st_=st_, tts=tts, m=mcols: e.tensor_tensor(
                    m, st_[:, :], col(tts, 0).broadcast_to((128, 8)), op=OP.is_ge),
                    r=[st_[:, :], col(tts, 0)], w=[mcols])
                C("vector", lambda e, tts=tts, m=mcols: e.tensor_scalar(
                    tts[:, 8:16], m, -1.0, 1.0, op0=OP.mult, op1=OP.add),
                    r=[mcols], w=[tts[:, 8:16]])
                C("vector", lambda e, tts=tts: e.tensor_tensor_scan(
                    tts[:, 8:16], tts[:, 8:16], ones8_sb[:, :], 1.0,
                    op0=OP.mult, op1=OP.mult),
                    r=[tts[:, 8:16], ones8_sb[:, :]], w=[tts[:, 8:16]])
                shm = C("vector", lambda e, tts=tts, mb_=mb_: e.tensor_tensor(
                    msk_sb[:, mb_ + 1:mb_ + 8], msk_sb[:, mb_ + 1:mb_ + 8],
                    tts[:, 8:15], op=OP.mult),
                    r=[mcols, tts[:, 8:16]], w=[msk_sb[:, mb_ + 1:mb_ + 8]])
                C("vector", lambda e, st_=st_, tts=tts, m=mcols: e.tensor_tensor(
                    tts[:, 8:16], m, st_[:, :], op=OP.mult),
                    r=[mcols, st_[:, :]], w=[tts[:, 8:16]])
                C("vector", lambda e, tts=tts: e.tensor_reduce(
                    col(tts, 1), tts[:, 8:16], axis=AX.X, op=OP.add),
                    r=[tts[:, 8:16]], w=[col(tts, 1)])
                sgm = C("scalar", lambda e, tts=tts, tt=tt, par=par: e.activation(
                    col(alf_sb, par * 4 + tt), col(tts, 1), AF.Sigmoid),
                    r=[col(tts, 1)], w=[col(alf_sb, par * 4 + tt)])
                mmarks.append((shm, sgm))
            msk_done[sc] = mmarks

        bias1_ready = [None] * 32

        def stage_D(sc):
            par = sc
            for tc in range(8):
                for kt in range(8):
                    DMA("sync", nhD[kt][:, :],
                        nh_out[par][tc * DM + kt * 128:
                                    tc * DM + (kt + 1) * 128, :])
                    C("vector", lambda e, kt=kt: e.tensor_scalar(
                        nhD[kt][:, :], nhD[kt][:, :], col(gam_sb, kt),
                        col(bet_sb, kt), op0=OP.mult, op1=OP.add),
                        r=[nhD[kt][:, :], col(gam_sb, kt), col(bet_sb, kt)],
                        w=[nhD[kt][:, :]])
                for hc in range(8):
                    for kt in range(8):
                        DMA("sync", w1c[kt][:, :], w1bf_d[hc, kt, :, :])
                    if sc == 0 and tc == 0:
                        for sub in range(4):
                            ht = hc * 4 + sub
                            for kt in range(8):
                                C("tensor", lambda e, kt=kt, sub=sub: e.matmul(
                                    PC1, w1c[kt][:, sub * 128:(sub + 1) * 128],
                                    col(betbf_sb, kt), start=(kt == 0),
                                    stop=(kt == 7)),
                                    r=[w1c[kt][:, sub * 128:(sub + 1) * 128],
                                       col(betbf_sb, kt)],
                                    w=[PC1])
                            bias1_ready[ht] = C(
                                "vector", lambda e, ht=ht: e.tensor_tensor(
                                    col(bias1_sb, ht), PC1, col(b1_sb, ht),
                                    op=OP.add),
                                r=[PC1, col(b1_sb, ht)], w=[col(bias1_sb, ht)])
                    for sub in range(4):
                        ht = hc * 4 + sub
                        PH = pm_b.acquire()
                        for kt in range(8):
                            C("tensor", lambda e, kt=kt, sub=sub, PH=PH: e.matmul(
                                PH[:, :], w1c[kt][:, sub * 128:(sub + 1) * 128],
                                nhD[kt][:, :], start=(kt == 0), stop=(kt == 7)),
                                r=[w1c[kt][:, sub * 128:(sub + 1) * 128],
                                   nhD[kt][:, :]],
                                w=[PH[:, :]])
                        C("scalar", lambda e, ht=ht, PH=PH: e.activation(
                            r1sb[ht][:, :], PH[:, :], AF.Relu,
                            bias=col(bias1_sb, ht)),
                            r=[PH[:, :], col(bias1_sb, ht)],
                            waits=[bias1_ready[ht]], w=[r1sb[ht][:, :]])
                for dh in range(2):
                    PWs = [pm_b.acquire() for _ in range(4)]
                    for half in range(2):
                        for hh in range(16):
                            ht = half * 16 + hh
                            DMA("sync", w2h[hh][:, :],
                                w2bf_d[ht * 128:(ht + 1) * 128,
                                       dh * 512:(dh + 1) * 512])
                        for tt in range(4):
                            for hh in range(16):
                                ht = half * 16 + hh
                                C("tensor", lambda e, ht=ht, hh=hh, tt=tt,
                                    PWs=PWs:
                                    e.matmul(
                                        PWs[tt][:, :],
                                        r1sb[ht][:, tt * 128:(tt + 1) * 128],
                                        w2h[hh][:, :], start=(ht == 0),
                                        stop=False),
                                    r=[r1sb[ht][:, :], w2h[hh][:, :]],
                                    w=[PWs[tt][:, :]])
                    for tt in range(4):
                        PW = PWs[tt]
                        C("tensor", lambda e, PW=PW, dh=dh: e.matmul(
                            PW[:, :], ones1_sb[:, :],
                            b2r_sb[:, dh * 512:(dh + 1) * 512],
                            start=False, stop=True),
                            r=[ones1_sb[:, :], b2r_sb[:, dh * 512:(dh + 1) * 512]],
                            w=[PW[:, :]])
                        ot = o2_b.acquire()
                        C("scalar", lambda e, ot=ot, PW=PW: e.activation(
                            ot[:, :], PW[:, :], AF.Copy), r=[PW[:, :]], w=[ot[:, :]])
                        DMA("gpsimd",
                            f_in[par][tc * 512 + tt * 128:tc * 512 + (tt + 1) * 128,
                                      dh * 512:(dh + 1) * 512],
                            ot[:, :])

            COLL("AllGather", OP.bypass, GROUP8,
                 f_in[par].ap(), f_out[par].ap())

        out_marks = []

        def stage_E(sc):
            par = sc
            for tt in range(4):
                at = acc_b.acquire()
                for ex in range(NEXP):
                    st_ = sl_b.acquire()
                    GATHER(st_[:, :], f_out[par][:, :], col(fidx_sb, ex * 4 + tt))
                    mc = col(msk_sb, (par * 4 + tt) * 8 + ex)
                    if ex == 0:
                        C("vector", lambda e, at=at, st_=st_, mc=mc:
                          e.tensor_scalar(at[:, :], st_[:, :], mc, None,
                                          op0=OP.mult),
                          r=[st_[:, :], mc], waits=[msk_done[sc][tt][0]],
                          w=[at[:, :]])
                    else:
                        C("vector", lambda e, at=at, st_=st_, mc=mc:
                          e.scalar_tensor_tensor(at[:, :], st_[:, :], mc,
                                                 at[:, :], op0=OP.mult, op1=OP.add),
                          r=[st_[:, :], mc, at[:, :]], w=[at[:, :]])
                wt = wE_b.acquire()
                DMA("sync", wt[:, :], w_d[par, tt * 128:(tt + 1) * 128, :])
                ac = col(alf_sb, par * 4 + tt)
                C("vector", lambda e, at=at, ac=ac: e.tensor_scalar(
                    at[:, :], at[:, :], ac, None, op0=OP.mult),
                    r=[at[:, :], ac], waits=[msk_done[sc][tt][1]], w=[at[:, :]])
                C("vector", lambda e, at=at, wt=wt: e.tensor_tensor(
                    at[:, :], at[:, :], wt[:, :], op=OP.add),
                    r=[at[:, :], wt[:, :]], w=[at[:, :]])
                r_c, mb_c = ln_pipeline(at[:, :], DM)
                C("vector", lambda e, at=at, r=r_c, mb=mb_c: e.tensor_scalar(
                    at[:, :], at[:, :], r, mb, op0=OP.mult, op1=OP.add),
                    r=[at[:, :], r_c, mb_c], w=[at[:, :]])
                xt = x1E_b.acquire()
                DMA("sync", xt[:, :], x1_d[sc, tt * 128:(tt + 1) * 128, :])
                C("vector", lambda e, xt=xt, at=at: e.tensor_tensor(
                    xt[:, :], xt[:, :], at[:, :], op=OP.add),
                    r=[xt[:, :], at[:, :]], w=[xt[:, :]])
                out_marks.append(
                    DMA("out", x2o[sc, tt * 128:(tt + 1) * 128, :], xt[:, :]))

        startup()
        stage_A(0)
        stage_A(1)
        stage_B(0)
        stage_C(0)
        stage_A(2)
        stage_B(1)
        stage_D(0)
        stage_C(1)
        stage_A(3)
        stage_B(2)
        stage_D(1)
        stage_E(0)
        stage_C(2)
        stage_B(3)
        stage_D(2)
        stage_E(1)
        stage_C(3)
        stage_D(3)
        stage_E(2)
        stage_E(3)

        if debug:
            fence = ([(s, s.v) for p in (qs_p, qg_p, qo_p, cc_p) for s in p.sems]
                     + [(sV, sV.v), (sS, sS.v), (sT, sT.v), (sG, sG.v)])
            for (o, t) in dbg_pairs:
                DMA("gpsimd", o.ap(), t.ap(), fence)
            for (nm, ap) in dbg_sbuf:
                DMA("gpsimd", dbg_sbuf_outs[nm][:, :], ap, fence)

        O("gpsimd", lambda e: e.nop(),
          waits=[out_marks[-1]]
          + [(s, s.v) for p in (qo_p, qg_p) for s in p.sems])

        block = E(nc.Block())
        hw = {e: {} for e in ENGS}

        def emitter(eng_name):
            def body(eng):
                seen = hw[eng_name]
                for (fn, ws, inc, inc_n) in ops[eng_name]:
                    for (s, v) in ws:
                        k = id(s)
                        if seen.get(k, 0) >= v:
                            continue
                        seen[k] = v
                        eng.wait_ge(s.h, v)
                    inst = fn(eng)
                    if inc is not None and inst is not None:
                        inst.then_inc(inc.h, inc_n)
            return body

        block.sync(emitter("sync"))
        block.scalar(emitter("scalar"))
        block.vector(emitter("vector"))
        block.tensor(emitter("tensor"))
        block.gpsimd(emitter("gpsimd"))

    return nc


_NC_CACHE = {}


def kernel(**inputs):
    x0 = np.asarray(inputs["L0_attn_id_seq"], np.float32)
    in_w = np.asarray(inputs["L0_attn_id_in_w"], np.float32)
    conv_w = np.asarray(inputs["L0_attn_id_conv_w"], np.float32)
    conv_b = np.asarray(inputs["L0_attn_id_conv_b"], np.float32)
    xproj = np.asarray(inputs["L0_attn_id_xproj_w"], np.float32)
    dt_w = np.asarray(inputs["L0_attn_id_dt_w"], np.float32)
    dt_b = np.asarray(inputs["L0_attn_id_dt_b"], np.float32)
    A_log = np.asarray(inputs["L0_attn_id_A_log"], np.float32)
    Dv = np.asarray(inputs["L0_attn_id_D"], np.float32)
    out_w = np.asarray(inputs["L0_attn_id_out_w"], np.float32)
    lng = np.asarray(inputs["L0_attn_id_ln_g"], np.float32)
    lnb = np.asarray(inputs["L0_attn_id_ln_b"], np.float32)
    cent = np.asarray(inputs["L0_ffn_centroids"], np.float32)
    gam = np.asarray(inputs["L0_ffn_gamma"], np.float32)
    bet = np.asarray(inputs["L0_ffn_beta"], np.float32)
    W1 = np.asarray(inputs["L0_ffn_W1"], np.float32)
    b1 = np.asarray(inputs["L0_ffn_b1"], np.float32)
    W2 = np.asarray(inputs["L0_ffn_W2"], np.float32)
    b2 = np.asarray(inputs["L0_ffn_b2"], np.float32)

    if "main" not in _NC_CACHE:
        _NC_CACHE["main"] = build(debug=False)
    nc = _NC_CACHE["main"]

    in_maps = []
    in_wT = np.ascontiguousarray(in_w.T)
    centT_h = np.ascontiguousarray(cent.T)
    for c in range(NCORES):
        b, q = c // 4, c % 4
        dq = q
        dsl = slice(dq * DSL, (dq + 1) * DSL)
        x0arr = np.stack([x0[b, sc * 2048 + q * 512: sc * 2048 + (q + 1) * 512]
                          for sc in range(NSC)])
        bidx_t = np.zeros((32, 128), np.uint32)
        for tch in range(4):
            for dt in range(4):
                base = tch * 2 * DI + dq * DSL + dt * 128
                bidx_t[tch * 8 + dt] = base + np.arange(128)
                bidx_t[tch * 8 + 4 + dt] = base + DI + np.arange(128)
        fidx_t = np.zeros((32, 128), np.uint32)
        for ex in range(8):
            for tt in range(4):
                fidx_t[ex * 4 + tt] = ex * GA + c * TSC + tt * 128 + np.arange(128)
        im = dict(
            x0c=np.ascontiguousarray(x0arr), in_wT=in_wT, ln_g=lng, ln_b=lnb,
            convw_sl=np.ascontiguousarray(conv_w[dsl]),
            convb_sl=np.ascontiguousarray(conv_b[dsl]),
            xprojT_sl=np.ascontiguousarray(xproj[:, dsl].T),
            dt_wT_sl=np.ascontiguousarray(dt_w[dsl].T),
            dt_b_sl=np.ascontiguousarray(dt_b[dsl]),
            A_log_sl=np.ascontiguousarray(A_log[dsl]),
            D_sl=np.ascontiguousarray(Dv[dsl]),
            out_wT_sl=np.ascontiguousarray(out_w[:, dsl].T),
            centT=centT_h,
            gam_e=np.ascontiguousarray(gam[c]), bet_e=np.ascontiguousarray(bet[c]),
            b1_e=np.ascontiguousarray(b1[c]), b2_e=np.ascontiguousarray(b2[c]),
            W1T_e=np.ascontiguousarray(W1[c].T), W2T_e=np.ascontiguousarray(W2[c].T),
            bidx=bidx_t, fidx=fidx_t,
        )
        in_maps.append(im)

    res = run_bass_kernel_spmd(nc, in_maps, core_ids=list(range(NCORES)))

    out = np.empty((2, 8192, 1024), np.float32)
    for c in range(NCORES):
        b, q = c // 4, c % 4
        r = np.asarray(res.results[c]["x2o"])
        for sc in range(NSC):
            out[b, sc * 2048 + q * 512: sc * 2048 + (q + 1) * 512] = r[sc]
    return out


# version 06 (final)
# speedup vs baseline: 1.1257x; 1.1257x over previous
import numpy as np
from contextlib import ExitStack

import concourse.bass as bass
import concourse.mybir as mybir
from concourse.bass_utils import run_bass_kernel_spmd

F32 = mybir.dt.float32
BF16 = mybir.dt.bfloat16
U32 = mybir.dt.uint32
AF = mybir.ActivationFunctionType
OP = mybir.AluOpType
AX = mybir.AxisListType

NCORES = 8
NSC = 4
TSC = 512
DM = 1024
DI = 2048
NST = 16
DTR = 64
DSL = 512
NEXP = 8
FF = 4096
GT = 4 * TSC
GA = 8 * TSC
EPS = 1e-5

ENGS = ("sync", "scalar", "vector", "tensor", "gpsimd")
GROUP4 = [[0, 1, 2, 3], [4, 5, 6, 7]]
GROUP8 = [list(range(8))]


class SemC:
    __slots__ = ("h", "v")

    def __init__(self, handle):
        self.h = handle
        self.v = 0


class Buf:

    def __init__(self, tiles):
        self.tiles = tiles
        self.i = 0

    def acquire(self):
        t = self.tiles[self.i % len(self.tiles)]
        self.i += 1
        return t


def build(debug=False):
    nc = bass.Bass("TRN2", target_bir_lowering=False)

    x0c = nc.dram_tensor("x0c", [NSC, TSC, DM], F32, kind="ExternalInput")
    in_wT = nc.dram_tensor("in_wT", [DM, 2 * DI], F32, kind="ExternalInput")
    ln_g = nc.dram_tensor("ln_g", [DM], F32, kind="ExternalInput")
    ln_b = nc.dram_tensor("ln_b", [DM], F32, kind="ExternalInput")
    convw_sl = nc.dram_tensor("convw_sl", [DSL, 4], F32, kind="ExternalInput")
    convb_sl = nc.dram_tensor("convb_sl", [DSL], F32, kind="ExternalInput")
    xprojT_sl = nc.dram_tensor("xprojT_sl", [DSL, 96], F32, kind="ExternalInput")
    dt_wT_sl = nc.dram_tensor("dt_wT_sl", [DTR, DSL], F32, kind="ExternalInput")
    dt_b_sl = nc.dram_tensor("dt_b_sl", [DSL], F32, kind="ExternalInput")
    A_log_sl = nc.dram_tensor("A_log_sl", [DSL, NST], F32, kind="ExternalInput")
    D_sl = nc.dram_tensor("D_sl", [DSL], F32, kind="ExternalInput")
    out_wT_sl = nc.dram_tensor("out_wT_sl", [DSL, DM], F32, kind="ExternalInput")
    centT = nc.dram_tensor("centT", [DM, NEXP], F32, kind="ExternalInput")
    gam_e = nc.dram_tensor("gam_e", [DM], F32, kind="ExternalInput")
    bet_e = nc.dram_tensor("bet_e", [DM], F32, kind="ExternalInput")
    b1_e = nc.dram_tensor("b1_e", [FF], F32, kind="ExternalInput")
    b2_e = nc.dram_tensor("b2_e", [DM], F32, kind="ExternalInput")
    W1T_e = nc.dram_tensor("W1T_e", [DM, FF], F32, kind="ExternalInput")
    W2T_e = nc.dram_tensor("W2T_e", [FF, DM], F32, kind="ExternalInput")
    bidx = nc.dram_tensor("bidx", [32, 128], U32, kind="ExternalInput")
    fidx = nc.dram_tensor("fidx", [32, 128], U32, kind="ExternalInput")

    x2o = nc.dram_tensor("x2o", [NSC, TSC, DM], F32, kind="ExternalOutput")

    dbg_pairs = []

    def scratch(name, shape, dt, dbg=False, space="Local"):
        t = nc.dram_tensor(name, shape, dt, kind="Internal", addr_space=space)
        if debug and dbg:
            o = nc.dram_tensor("dbg_" + name, shape, dt, kind="ExternalOutput")
            dbg_pairs.append((o, t))
        return t

    u_d = scratch("u_d", [NSC, TSC, DM], F32, dbg=True)
    e1_in = [scratch(f"e1_in{p}", [2 * DI, TSC], BF16, dbg=True) for p in range(NSC)]
    e1_out = [scratch(f"e1_out{p}", [4 * 2 * DI, TSC], BF16) for p in range(NSC)]
    dbl_in = [scratch(f"dbl_in{p}", [4, 96, TSC], F32) for p in range(NSC)]
    dbl_out = [scratch(f"dbl_out{p}", [4, 96, TSC], F32, dbg=True) for p in range(NSC)]
    rs_in = [scratch(f"rs_in{p}", [GT, DM], BF16) for p in range(NSC)]
    rs_out = [scratch(f"rs_out{p}", [TSC, DM], BF16, dbg=True) for p in range(NSC)]
    nh_in = [scratch(f"nh_in{p}", [DM, TSC], BF16) for p in range(NSC)]
    nh_out = [scratch(f"nh_out{p}", [8 * DM, TSC], BF16, dbg=True, space="Shared")
              for p in range(NSC)]
    f_in = [scratch(f"f_in{p}", [GA, DM], BF16, dbg=True) for p in range(NSC)]
    f_out = [scratch(f"f_out{p}", [8 * GA, DM], BF16, dbg=True, space="Shared")
             for p in range(NSC)]
    w1bf_d = scratch("w1bf_d", [8, 8, 128, 512], BF16)
    w2bf_d = scratch("w2bf_d", [FF, DM], BF16)
    x1_d = scratch("x1_d", [NSC, TSC, DM], F32, dbg=True)
    w_d = scratch("w_d", [NSC, TSC, DM], F32, dbg=True)

    ops = {e: [] for e in ENGS}
    dbg_sbuf = []
    dbg_sbuf_outs = {}

    with ExitStack() as ctx:
        E = ctx.enter_context

        sems = {n: SemC(E(nc.semaphore(n)))
                for n in ("sS", "sV", "sT", "sG")}
        sS, sV, sT, sG = sems["sS"], sems["sV"], sems["sT"], sems["sG"]
        esem = {"scalar": sS, "vector": sV, "tensor": sT, "gpsimd": sG, "sync": None}

        class QPool:

            def __init__(self, name, k):
                self.sems = [SemC(E(nc.semaphore(f"{name}{i}"))) for i in range(k)]
                self.i = 0

            def next(self):
                s = self.sems[self.i % len(self.sems)]
                self.i += 1
                prev = (s, s.v) if s.v else None
                return s, prev

        qs_p = QPool("qs", 12)
        qg_p = QPool("qg", 12)
        qo_p = QPool("qo", 6)
        cc_p = QPool("cc", 6)

        trk_w = {}
        trk_r = {}

        def O(eng, fn, r=(), w=(), waits=(), inc=None, inc_n=1):
            deps = {}

            def add(m):
                if m is None:
                    return
                s, v = m
                k = id(s)
                if k not in deps or deps[k][1] < v:
                    deps[k] = (s, v)

            for m in waits:
                add(m)
            rn = [ap.tensor.name for ap in r]
            wn = [ap.tensor.name for ap in w]
            for n in rn:
                add(trk_w.get(n))
            for n in wn:
                add(trk_w.get(n))
                for m in trk_r.get(n, {}).values():
                    add(m)
            ws = []
            for (s, v) in deps.values():
                if eng == "tensor" and s is sT:
                    continue
                ws.append((s, v))
            if inc is not None:
                inc.v += inc_n
                mark = (inc, inc.v)
            else:
                mark = None
            ops[eng].append((fn, ws, inc, inc_n if inc else 0))
            if mark is not None:
                for n in rn:
                    d = trk_r.setdefault(n, {})
                    k = id(mark[0])
                    if k not in d or d[k][1] < mark[1]:
                        d[k] = mark
                for n in wn:
                    trk_w[n] = mark
                    trk_r[n] = {}
            return mark

        def C(eng, fn, r=(), w=(), waits=()):
            return O(eng, fn, r, w, waits, inc=esem[eng], inc_n=1)

        def DMA(queue, out_ap, in_ap, waits=(), **kw):
            if queue == "sync":
                pool, eng = qs_p, "sync"
            elif queue == "out":
                pool, eng = qo_p, "gpsimd"
            else:
                pool, eng = qg_p, "gpsimd"
            s, prev = pool.next()

            def f(e, o=out_ap, i=in_ap, kw=kw):
                return e.dma_start(o, i, **kw)
            return O(eng, f, r=[in_ap], w=[out_ap],
                     waits=list(waits) + [prev], inc=s, inc_n=16)

        def GATHER(out_ap, in_ap, idx_ap, waits=()):
            s, prev = qg_p.next()

            def f(e, o=out_ap, i=in_ap, x=idx_ap):
                return e.indirect_dma_start(
                    out=o, out_offset=None, in_=i,
                    in_offset=bass.IndirectOffsetOnAxis(ap=x, axis=0))
            return O("gpsimd", f, r=[in_ap, idx_ap], w=[out_ap],
                     waits=list(waits) + [prev], inc=s, inc_n=16)

        def COLL(kind, op, groups, in_ap, out_ap, waits=()):
            s, prev = cc_p.next()

            def f(e, k=kind, o=op, g=groups, i=in_ap, u=out_ap):
                return e.collective_compute(k, o, replica_groups=g, ins=[i], outs=[u])
            return O("gpsimd", f, r=[in_ap], w=[out_ap],
                     waits=list(waits) + [prev], inc=s, inc_n=1)

        def sb(name, p, f, dt):
            return E(nc.sbuf_tensor(name, [p, f], dt))

        def ring(name, n, p, f, dt):
            return Buf([sb(f"{name}{i}", p, f, dt) for i in range(n)])

        idn_bf = sb("idn_bf", 128, 128, BF16)
        idn_f = sb("idn_f", 128, 128, F32)
        iota_f = sb("iota_f", 128, 128, F32)
        xpsl_sb = sb("xpsl_sb", 128, 4 * 96, BF16)
        dtwsl_sb = sb("dtwsl_sb", 64, DSL, BF16)
        A_sb = sb("A_sb", 128, 64, F32)
        convw_sb = sb("convw_sb", 128, 16, F32)
        convb_sb = sb("convb_sb", 128, 4, F32)
        dtb_sb = sb("dtb_sb", 128, 4, F32)
        Dv_sb = sb("Dv_sb", 128, 4, F32)
        lng_sb = sb("lng_sb", 128, 8, F32)
        lnb_sb = sb("lnb_sb", 128, 8, F32)
        gam_sb = sb("gam_sb", 128, 8, F32)
        bet_sb = sb("bet_sb", 128, 8, F32)
        betbf_sb = sb("betbf_sb", 128, 8, BF16)
        b1_sb = sb("b1_sb", 128, 32, F32)
        bias1_sb = sb("bias1_sb", 128, 32, F32)
        b2r_sb = sb("b2r_sb", 1, DM, BF16)
        ones1_sb = sb("ones1_sb", 1, 128, BF16)
        ones8_sb = sb("ones8_sb", 128, 8, F32)
        centT_sb = sb("centT_sb", 128, 64, F32)
        state_sb = sb("state_sb", 128, 64, F32)
        bidx_sb = sb("bidx_sb", 128, 32, U32)
        fidx_sb = sb("fidx_sb", 128, 32, U32)
        diag_sb = sb("diag_sb", 128, 16 * 128, BF16)
        outw_sb = sb("outw_sb", 128, 4 * DM, BF16)
        epsone_sb = sb("epsone_sb", 128, 2, F32)
        trash = sb("trash", 128, DM, BF16)

        x0t_b = ring("x0t", 2, 128, DM, F32)
        colb = ring("colb", 3, 128, 8, F32)
        u_b = ring("u_b", 2, 128, DM, F32)
        lnu_b = ring("lnu", 1, 128, DM, BF16)
        LNuT = sb("LNuT", 128, 8 * TSC, BF16)
        inwj_b = ring("inwj", 2, 128, DM, BF16)
        xzev_b = ring("xzev", 2, 128, TSC, BF16)

        convin = [sb(f"convin{i}", 128, 3 + TSC, BF16) for i in range(4)]
        xi2blk = [sb(f"xi2blk{i}", 128, TSC, BF16) for i in range(4)]
        dtT_b = ring("dtTb", 2, 64, TSC, BF16)
        sp_b = ring("spb", 2, 128, TSC, F32)
        dlblk = [sb(f"dlblk{i}", 128, TSC, BF16) for i in range(4)]
        dublk = [sb(f"dublk{i}", 128, TSC, BF16) for i in range(4)]
        brep_b = ring("brep", 2, 128, TSC, BF16)
        crep_b = ring("crep", 2, 128, TSC, BF16)
        dA_b = ring("dAb", 2, 128, TSC, F32)
        X_b = ring("Xb", 2, 128, TSC, BF16)
        H_b = ring("Hb", 2, 128, TSC, BF16)
        P_b = ring("Pb", 2, 128, TSC, BF16)
        yacc = [sb(f"yacc{i}", 128, TSC, BF16) for i in range(4)]
        zblk_b = ring("zblk", 1, 128, TSC, BF16)
        szb_b = ring("szb", 1, 128, TSC, BF16)
        y2blk = [sb(f"y2blk{i}", 128, TSC, BF16) for i in range(4)]
        rsev_b = ring("rsev", 2, 128, TSC, BF16)

        m_b = ring("m_b", 1, 128, DM, BF16)
        aC_b = ring("aC", 1, 128, DM, F32)
        x0m_b = ring("x0m", 1, 128, DM, F32)
        nhat_b = ring("nhat", 1, 128, DM, BF16)
        wT_b = ring("wTb", 4, 128, 128, F32)
        nhT_b = ring("nhT", 2, 128, 128, BF16)
        scm_b = ring("scm", 4, 128, 8, F32)
        sct_b = ring("sct", 2, 128, 16, F32)
        msk_sb = sb("msk_sb", 128, 4 * 4 * 8, F32)
        alf_sb = sb("alf_sb", 128, 32, F32)

        w1c = [sb(f"w1c{i}", 128, 512, BF16) for i in range(16)]
        nhD = [sb(f"nhD{i}", 128, 512, BF16) for i in range(8)]
        w2h = [sb(f"w2h{i}", 128, 512, BF16) for i in range(16)]
        r1sb = [sb(f"r1sb{i}", 128, 512, BF16) for i in range(32)]
        o2_b = ring("o2b", 2, 128, 512, BF16)

        sl_b = ring("slb", 2, 128, DM, BF16)
        acc_b = ring("accb", 1, 128, DM, F32)
        wE_b = ring("wEb", 1, 128, DM, F32)
        x1E_b = ring("x1E", 1, 128, DM, F32)

        if debug:
            for nm, ap, shp, dt in [
                ("bias1", bias1_sb[:, :], [128, 32], F32),
                ("msk", msk_sb[:, :], [128, 128], F32),
                ("alf", alf_sb[:, :], [128, 32], F32),
                ("A", A_sb[:, :], [128, 64], F32),
                ("state", state_sb[:, :], [128, 64], F32),
            ]:
                dbg_sbuf.append((nm, ap))
                dbg_sbuf_outs[nm] = nc.dram_tensor("dbgsb_" + nm, shp, dt,
                                                   kind="ExternalOutput")

        PM = [E(nc.psum_tensor(f"pm{i}", [128, 512], F32)) for i in range(6)]
        pm_b = Buf(PM)
        PX = E(nc.psum_tensor("px", [96, 512], F32))
        PAUX = E(nc.psum_tensor("paux", [128, 512], F32))
        PTF = PAUX[:, 0:128]
        PT = PAUX[:, 128:192].bitcast(BF16)
        PC1 = PAUX[:, 192:193]

        def col(t, i):
            return t[:, i:i + 1]

        def ln_pipeline(x_ap, fdim):
            ct = colb.acquire()
            mean_c, m2_c, var_c, r_c, mb_c = (col(ct, k) for k in range(5))
            inv = 1.0 / fdim
            sq = float(1.0 / np.sqrt(fdim))
            C("scalar", lambda e: e.activation(
                trash[:, :fdim], x_ap, AF.Copy, scale=inv, accum_out=mean_c),
                r=[x_ap], w=[trash[:, :fdim], mean_c])
            C("scalar", lambda e: e.activation(
                trash[:, :fdim], x_ap, AF.Square, scale=sq, accum_out=m2_c),
                r=[x_ap], w=[trash[:, :fdim], m2_c])
            C("vector", lambda e: e.tensor_scalar(
                var_c, mean_c, mean_c, -1.0, op0=OP.mult, op1=OP.mult),
                r=[mean_c], w=[var_c])
            C("vector", lambda e: e.tensor_tensor(var_c, var_c, m2_c, op=OP.add),
                r=[var_c, m2_c], w=[var_c])
            C("scalar", lambda e: e.activation(var_c, var_c, AF.Ln, bias=EPS),
                r=[var_c, epsone_sb[:, 0:1]], w=[var_c])
            C("scalar", lambda e: e.activation(r_c, var_c, AF.Exp, scale=-0.5),
                r=[var_c], w=[r_c])
            C("vector", lambda e: e.tensor_scalar(
                mb_c, mean_c, r_c, -1.0, op0=OP.mult, op1=OP.mult),
                r=[mean_c, r_c], w=[mb_c])
            return r_c, mb_c

        def startup():
            C("vector", lambda e: e.memset(epsone_sb[:, 0:1], EPS),
              w=[epsone_sb[:, 0:1]])
            nc.const_aps.register(F32, EPS, epsone_sb[:, 0:1])
            C("vector", lambda e: e.memset(epsone_sb[:, 1:2], 1.0),
              w=[epsone_sb[:, 1:2]])
            nc.const_aps.register(F32, 1.0, epsone_sb[:, 1:2])
            C("gpsimd", lambda e: e.iota(iota_f[:, :], [[1, 128]], base=0,
                                         channel_multiplier=-1,
                                         allow_small_or_imprecise_dtypes=True),
              w=[iota_f[:, :]])
            C("vector", lambda e: e.tensor_scalar(
                idn_f[:, :], iota_f[:, :], 0.0, None, op0=OP.is_equal),
                r=[iota_f[:, :]], w=[idn_f[:, :]])
            C("vector", lambda e: e.tensor_copy(idn_bf[:, :], idn_f[:, :]),
                r=[idn_f[:, :]], w=[idn_bf[:, :]])
            C("vector", lambda e: e.memset(ones8_sb[:, :], 1.0), w=[ones8_sb[:, :]])
            C("vector", lambda e: e.memset(state_sb[:, :], 0.0), w=[state_sb[:, :]])
            C("gpsimd", lambda e: e.memset(ones1_sb[:, :], 1.0), w=[ones1_sb[:, :]])
            for dt in range(4):
                C("vector", lambda e, dt=dt: e.memset(convin[dt][:, 0:3], 0.0),
                  w=[convin[dt][:, 0:3]])

            DMA("gpsimd", xpsl_sb[:, :],
                bass.AP(xprojT_sl, 0, [[96, 128], [96 * 128, 4], [1, 96]]))
            DMA("gpsimd", dtwsl_sb[:, :], dt_wT_sl[:, :])
            DMA("sync", A_sb[:, :],
                bass.AP(A_log_sl, 0, [[16, 128], [16 * 128, 4], [1, 16]]))
            DMA("sync", convw_sb[:, :],
                bass.AP(convw_sl, 0, [[4, 128], [4 * 128, 4], [1, 4]]))
            slow = dict(allow_slow_non_contiguous=True)
            DMA("sync", convb_sb[:, :], bass.AP(convb_sl, 0, [[1, 128], [128, 4]]), **slow)
            DMA("sync", dtb_sb[:, :], bass.AP(dt_b_sl, 0, [[1, 128], [128, 4]]), **slow)
            DMA("sync", Dv_sb[:, :], bass.AP(D_sl, 0, [[1, 128], [128, 4]]), **slow)
            DMA("sync", lng_sb[:, :], bass.AP(ln_g, 0, [[1, 128], [128, 8]]), **slow)
            DMA("sync", lnb_sb[:, :], bass.AP(ln_b, 0, [[1, 128], [128, 8]]), **slow)
            DMA("sync", gam_sb[:, :], bass.AP(gam_e, 0, [[1, 128], [128, 8]]), **slow)
            DMA("sync", bet_sb[:, :], bass.AP(bet_e, 0, [[1, 128], [128, 8]]), **slow)
            C("vector", lambda e: e.tensor_copy(betbf_sb[:, :], bet_sb[:, :]),
              r=[bet_sb[:, :]], w=[betbf_sb[:, :]])
            DMA("sync", b1_sb[:, :], bass.AP(b1_e, 0, [[1, 128], [128, 32]]), **slow)
            DMA("gpsimd", b2r_sb[:, :], b2_e[None, :])
            DMA("sync", centT_sb[:, :],
                bass.AP(centT, 0, [[8, 128], [8 * 128, 8], [1, 8]]))
            DMA("sync", bidx_sb[:, :], bass.AP(bidx, 0, [[1, 128], [128, 32]]), **slow)
            DMA("sync", fidx_sb[:, :], bass.AP(fidx, 0, [[1, 128], [128, 32]]), **slow)

            C("scalar", lambda e: e.activation(A_sb[:, :], A_sb[:, :], AF.Exp),
              r=[A_sb[:, :]], w=[A_sb[:, :]])
            C("vector", lambda e: e.tensor_scalar(
                A_sb[:, :], A_sb[:, :], -1.0, None, op0=OP.mult),
                r=[A_sb[:, :]], w=[A_sb[:, :]])

            for c0 in range(16):
                C("vector", lambda e, c0=c0: e.tensor_scalar(
                    diag_sb[:, c0 * 128:(c0 + 1) * 128], idn_f[:, :],
                    col(convw_sb, c0), None, op0=OP.mult),
                    r=[idn_f[:, :], col(convw_sb, c0)],
                    w=[diag_sb[:, c0 * 128:(c0 + 1) * 128]])

            for dt in range(4):
                DMA("gpsimd", outw_sb[:, dt * DM:(dt + 1) * DM],
                    out_wT_sl[dt * 128:(dt + 1) * 128, :])

            for ht in range(32):
                t = r1sb[ht]
                DMA("gpsimd", t[:, 0:512], W2T_e[ht * 128:(ht + 1) * 128, 0:512])
                DMA("gpsimd", w2bf_d[ht * 128:(ht + 1) * 128, 0:512], t[:, 0:512])
                t2 = w2h[ht % 16]
                DMA("gpsimd", t2[:, 0:512], W2T_e[ht * 128:(ht + 1) * 128, 512:DM])
                DMA("gpsimd", w2bf_d[ht * 128:(ht + 1) * 128, 512:DM], t2[:, 0:512])
            for hc in range(8):
                for kt in range(8):
                    t = r1sb[(hc * 8 + kt) % 32]
                    DMA("gpsimd", t[:, 0:512],
                        bass.AP(W1T_e, kt * 128 * FF + hc * 512,
                                [[FF, 128], [1, 512]]))
                    DMA("gpsimd", w1bf_d[hc, kt, :, :], t[:, 0:512])

        def stage_A(sc):
            par = sc
            for ti in range(4):
                xt = x0t_b.acquire()
                DMA("sync", xt[:, :], x0c[sc, ti * 128:(ti + 1) * 128, :])
                r_c, mb_c = ln_pipeline(xt[:, :], DM)
                ut = u_b.acquire()
                C("vector", lambda e, xt=xt, ut=ut, r=r_c, mb=mb_c: e.tensor_scalar(
                    ut[:, :], xt[:, :], r, mb, op0=OP.mult, op1=OP.add),
                    r=[xt[:, :], r_c, mb_c], w=[ut[:, :]])
                DMA("out", u_d[sc, ti * 128:(ti + 1) * 128, :], ut[:, :])
                lt = lnu_b.acquire()
                C("vector", lambda e, ut=ut, lt=lt: e.tensor_copy(lt[:, :], ut[:, :]),
                    r=[ut[:, :]], w=[lt[:, :]])
                for kt in range(8):
                    C("tensor", lambda e, lt=lt, kt=kt: e.transpose(
                        PT, lt[:, kt * 128:(kt + 1) * 128], idn_bf[:, :]),
                        r=[lt[:, kt * 128:(kt + 1) * 128], idn_bf[:, :]], w=[PT])
                    C("vector", lambda e, kt=kt, ti=ti: e.tensor_copy(
                        LNuT[:, kt * TSC + ti * 128: kt * TSC + (ti + 1) * 128], PT),
                        r=[PT],
                        w=[LNuT[:, kt * TSC + ti * 128: kt * TSC + (ti + 1) * 128]])

            for kt in range(8):
                C("vector", lambda e, kt=kt: e.tensor_scalar(
                    LNuT[:, kt * TSC:(kt + 1) * TSC], LNuT[:, kt * TSC:(kt + 1) * TSC],
                    col(lng_sb, kt), col(lnb_sb, kt), op0=OP.mult, op1=OP.add),
                    r=[LNuT[:, kt * TSC:(kt + 1) * TSC], col(lng_sb, kt),
                       col(lnb_sb, kt)],
                    w=[LNuT[:, kt * TSC:(kt + 1) * TSC]])

            for j in range(32):
                wt = inwj_b.acquire()
                DMA("gpsimd", wt[:, :],
                    bass.AP(in_wT, j * 128,
                            [[2 * DI, 128], [128 * 2 * DI, 8], [1, 128]]))
                PA = pm_b.acquire()
                for kt in range(8):
                    C("tensor", lambda e, kt=kt, wt=wt, PA=PA: e.matmul(
                        PA[:, :], wt[:, kt * 128:(kt + 1) * 128],
                        LNuT[:, kt * TSC:(kt + 1) * TSC],
                        start=(kt == 0), stop=(kt == 7)),
                        r=[wt[:, kt * 128:(kt + 1) * 128],
                           LNuT[:, kt * TSC:(kt + 1) * TSC]],
                        w=[PA[:, :]])
                zt = xzev_b.acquire()
                C("scalar", lambda e, zt=zt, PA=PA: e.activation(
                    zt[:, :], PA[:, :], AF.Copy), r=[PA[:, :]], w=[zt[:, :]])
                DMA("gpsimd", e1_in[par][j * 128:(j + 1) * 128, :], zt[:, :])

            COLL("AllGather", OP.bypass, GROUP4,
                 e1_in[par].ap(), e1_out[par].ap())

        def stage_B(sc):
            par = sc
            for tch in range(4):
                for dt in range(4):
                    GATHER(convin[dt][:, 3:3 + TSC], e1_out[par][:, :],
                           col(bidx_sb, tch * 8 + dt))
                    PC = pm_b.acquire()
                    for tap in range(4):
                        C("tensor", lambda e, dt=dt, tap=tap, PC=PC: e.matmul(
                            PC[:, :],
                            diag_sb[:, (dt * 4 + tap) * 128:(dt * 4 + tap + 1) * 128],
                            convin[dt][:, tap:tap + TSC],
                            start=(tap == 0), stop=(tap == 3)),
                            r=[diag_sb[:, (dt * 4 + tap) * 128:
                                       (dt * 4 + tap + 1) * 128],
                               convin[dt][:, :]],
                            w=[PC[:, :]])
                    C("vector", lambda e, dt=dt: e.tensor_copy(
                        convin[dt][:, 0:3], convin[dt][:, TSC:TSC + 3]),
                        r=[convin[dt][:, :]], w=[convin[dt][:, 0:3]])
                    C("scalar", lambda e, dt=dt, PC=PC: e.activation(
                        xi2blk[dt][:, :], PC[:, :], AF.Silu, bias=col(convb_sb, dt)),
                        r=[PC[:, :], col(convb_sb, dt)], w=[xi2blk[dt][:, :]])
                    C("tensor", lambda e, dt=dt: e.matmul(
                        PX[:, :], xpsl_sb[:, dt * 96:(dt + 1) * 96], xi2blk[dt][:, :],
                        start=(dt == 0), stop=(dt == 3)),
                        r=[xpsl_sb[:, dt * 96:(dt + 1) * 96], xi2blk[dt][:, :]],
                        w=[PX[:, :]])
                st_ = sp_b.acquire()
                C("scalar", lambda e, st_=st_: e.activation(
                    st_[0:96, 0:TSC], PX[:, :], AF.Copy),
                    r=[PX[:, :]], w=[st_[:, :]])
                DMA("gpsimd", dbl_in[par][tch, :, :], st_[0:96, 0:TSC])
                COLL("AllReduce", OP.add, GROUP4,
                     dbl_in[par][tch, :, :], dbl_out[par][tch, :, :])

                dtt = dtT_b.acquire()
                DMA("gpsimd", dtt[:, :], dbl_out[par][tch, 0:64, :])
                for mt in range(4):
                    PB = pm_b.acquire()
                    C("tensor", lambda e, mt=mt, PB=PB, dtt=dtt: e.matmul(
                        PB[:, :], dtwsl_sb[:, mt * 128:(mt + 1) * 128], dtt[:, :],
                        start=True, stop=True),
                        r=[dtwsl_sb[:, mt * 128:(mt + 1) * 128], dtt[:, :]],
                        w=[PB[:, :]])
                    s1 = sp_b.acquire()
                    s2 = sp_b.acquire()
                    C("scalar", lambda e, s1=s1, PB=PB, mt=mt: e.activation(
                        s1[:, :], PB[:, :], AF.Abs, bias=col(dtb_sb, mt)),
                        r=[PB[:, :], col(dtb_sb, mt)], w=[s1[:, :]])
                    C("scalar", lambda e, s1=s1: e.activation(
                        s1[:, :], s1[:, :], AF.Exp, scale=-1.0),
                        r=[s1[:, :]], w=[s1[:, :]])
                    C("scalar", lambda e, s1=s1: e.activation(
                        s1[:, :], s1[:, :], AF.Ln, bias=1.0),
                        r=[s1[:, :], epsone_sb[:, 1:2]], w=[s1[:, :]])
                    C("scalar", lambda e, s2=s2, PB=PB, mt=mt: e.activation(
                        s2[:, :], PB[:, :], AF.Relu, bias=col(dtb_sb, mt)),
                        r=[PB[:, :], col(dtb_sb, mt)], w=[s2[:, :]])
                    C("vector", lambda e, mt=mt, s1=s1, s2=s2: e.tensor_tensor(
                        dlblk[mt][:, :], s1[:, :], s2[:, :], op=OP.add),
                        r=[s1[:, :], s2[:, :]], w=[dlblk[mt][:, :]])
                    C("vector", lambda e, mt=mt: e.tensor_tensor(
                        dublk[mt][:, :], dlblk[mt][:, :], xi2blk[mt][:, :],
                        op=OP.mult),
                        r=[dlblk[mt][:, :], xi2blk[mt][:, :]], w=[dublk[mt][:, :]])

                for n in range(NST):
                    bt = brep_b.acquire()
                    DMA("gpsimd", bt[:, :],
                        bass.AP(dbl_out[par], tch * 96 * TSC + (64 + n) * TSC,
                                [[0, 128], [1, TSC]]))
                    ct2 = crep_b.acquire()
                    DMA("gpsimd", ct2[:, :],
                        bass.AP(dbl_out[par], tch * 96 * TSC + (80 + n) * TSC,
                                [[0, 128], [1, TSC]]))
                    for dt in range(4):
                        at = dA_b.acquire()
                        C("scalar", lambda e, at=at, dt=dt, n=n: e.activation(
                            at[:, :], dlblk[dt][:, :], AF.Exp,
                            scale=col(A_sb, dt * 16 + n)),
                            r=[dlblk[dt][:, :], col(A_sb, dt * 16 + n)],
                            w=[at[:, :]])
                        xt2 = X_b.acquire()
                        C("vector", lambda e, xt2=xt2, dt=dt, bt=bt: e.tensor_tensor(
                            xt2[:, :], dublk[dt][:, :], bt[:, :], op=OP.mult),
                            r=[dublk[dt][:, :], bt[:, :]], w=[xt2[:, :]])
                        ht_ = H_b.acquire()
                        scol = col(state_sb, n * 4 + dt)
                        C("vector", lambda e, ht_=ht_, at=at, xt2=xt2, s=scol:
                          e.tensor_tensor_scan(ht_[:, :], at[:, :], xt2[:, :], s,
                                               op0=OP.mult, op1=OP.add),
                          r=[at[:, :], xt2[:, :], scol], w=[ht_[:, :]])
                        C("vector", lambda e, ht_=ht_, s=scol: e.tensor_copy(
                            s, ht_[:, TSC - 1:TSC]),
                            r=[ht_[:, :]], w=[scol])
                        if n == 0:
                            C("vector", lambda e, dt=dt, ht_=ht_, ct2=ct2:
                              e.tensor_tensor(yacc[dt][:, :], ht_[:, :],
                                              ct2[:, :], op=OP.mult),
                              r=[ht_[:, :], ct2[:, :]], w=[yacc[dt][:, :]])
                        else:
                            pt3 = P_b.acquire()
                            C("vector", lambda e, pt3=pt3, ht_=ht_, ct2=ct2:
                              e.tensor_tensor(pt3[:, :], ht_[:, :], ct2[:, :],
                                              op=OP.mult),
                              r=[ht_[:, :], ct2[:, :]], w=[pt3[:, :]])
                            C("vector", lambda e, dt=dt, pt3=pt3: e.tensor_tensor(
                                yacc[dt][:, :], yacc[dt][:, :], pt3[:, :],
                                op=OP.add),
                                r=[yacc[dt][:, :], pt3[:, :]], w=[yacc[dt][:, :]])

                for dt in range(4):
                    zt = zblk_b.acquire()
                    GATHER(zt[:, :], e1_out[par][:, :],
                           col(bidx_sb, tch * 8 + 4 + dt))
                    szt = szb_b.acquire()
                    C("scalar", lambda e, szt=szt, zt=zt: e.activation(
                        szt[:, :], zt[:, :], AF.Silu),
                        r=[zt[:, :]], w=[szt[:, :]])
                    C("vector", lambda e, dt=dt: e.tensor_scalar(
                        y2blk[dt][:, :], xi2blk[dt][:, :], col(Dv_sb, dt), None,
                        op0=OP.mult),
                        r=[xi2blk[dt][:, :], col(Dv_sb, dt)], w=[y2blk[dt][:, :]])
                    C("vector", lambda e, dt=dt: e.tensor_tensor(
                        y2blk[dt][:, :], y2blk[dt][:, :], yacc[dt][:, :], op=OP.add),
                        r=[y2blk[dt][:, :], yacc[dt][:, :]], w=[y2blk[dt][:, :]])
                    C("vector", lambda e, dt=dt, szt=szt: e.tensor_tensor(
                        y2blk[dt][:, :], y2blk[dt][:, :], szt[:, :], op=OP.mult),
                        r=[y2blk[dt][:, :], szt[:, :]], w=[y2blk[dt][:, :]])

                for tt in range(4):
                    for dh in range(2):
                        PO = pm_b.acquire()
                        for dt in range(4):
                            C("tensor", lambda e, dt=dt, tt=tt, dh=dh, PO=PO:
                              e.matmul(
                                  PO[:, :], y2blk[dt][:, tt * 128:(tt + 1) * 128],
                                  outw_sb[:, dt * DM + dh * 512:
                                          dt * DM + (dh + 1) * 512],
                                  start=(dt == 0), stop=(dt == 3)),
                              r=[y2blk[dt][:, :],
                                 outw_sb[:, dt * DM + dh * 512:
                                         dt * DM + (dh + 1) * 512]],
                              w=[PO[:, :]])
                        rt = rsev_b.acquire()
                        C("scalar", lambda e, rt=rt, PO=PO: e.activation(
                            rt[:, :], PO[:, :], AF.Copy),
                            r=[PO[:, :]], w=[rt[:, :]])
                        DMA("gpsimd",
                            rs_in[par][tch * 512 + tt * 128:
                                       tch * 512 + (tt + 1) * 128,
                                       dh * 512:(dh + 1) * 512],
                            rt[:, :])

            COLL("ReduceScatter", OP.add, GROUP4,
                 rs_in[par].ap(), rs_out[par].ap())

        msk_done = [None] * NSC

        def stage_C(sc):
            par = sc
            sc_tiles = []
            for tt in range(4):
                mt_ = m_b.acquire()
                DMA("sync", mt_[:, :], rs_out[par][tt * 128:(tt + 1) * 128, :])
                at = aC_b.acquire()
                DMA("sync", at[:, :], u_d[sc, tt * 128:(tt + 1) * 128, :])
                C("vector", lambda e, at=at, mt_=mt_: e.scalar_tensor_tensor(
                    at[:, :], mt_[:, :], 0.1, at[:, :], op0=OP.mult, op1=OP.add),
                    r=[mt_[:, :], at[:, :]], w=[at[:, :]])
                r_c, mb_c = ln_pipeline(at[:, :], DM)
                C("vector", lambda e, at=at, r=r_c, mb=mb_c: e.tensor_scalar(
                    at[:, :], at[:, :], r, mb, op0=OP.mult, op1=OP.add),
                    r=[at[:, :], r_c, mb_c], w=[at[:, :]])
                xt = x0m_b.acquire()
                DMA("sync", xt[:, :], x0c[sc, tt * 128:(tt + 1) * 128, :])
                C("vector", lambda e, xt=xt, at=at: e.tensor_tensor(
                    xt[:, :], xt[:, :], at[:, :], op=OP.add),
                    r=[xt[:, :], at[:, :]], w=[xt[:, :]])
                DMA("out", x1_d[sc, tt * 128:(tt + 1) * 128, :], xt[:, :])
                r2, mb2 = ln_pipeline(xt[:, :], DM)
                C("vector", lambda e, at=at, xt=xt, r=r2, mb=mb2: e.tensor_scalar(
                    at[:, :], xt[:, :], r, mb, op0=OP.mult, op1=OP.add),
                    r=[xt[:, :], r2, mb2], w=[at[:, :]])
                DMA("out", w_d[par, tt * 128:(tt + 1) * 128, :], at[:, :])
                r3, mb3 = ln_pipeline(at[:, :], DM)
                nt_ = nhat_b.acquire()
                C("vector", lambda e, nt_=nt_, at=at, r=r3, mb=mb3: e.tensor_scalar(
                    nt_[:, :], at[:, :], r, mb, op0=OP.mult, op1=OP.add),
                    r=[at[:, :], r3, mb3], w=[nt_[:, :]])
                for kt in range(8):
                    C("tensor", lambda e, nt_=nt_, kt=kt: e.transpose(
                        PT, nt_[:, kt * 128:(kt + 1) * 128], idn_bf[:, :]),
                        r=[nt_[:, kt * 128:(kt + 1) * 128], idn_bf[:, :]], w=[PT])
                    nt2 = nhT_b.acquire()
                    C("vector", lambda e, nt2=nt2: e.tensor_copy(nt2[:, :], PT),
                        r=[PT], w=[nt2[:, :]])
                    DMA("gpsimd",
                        nh_in[par][kt * 128:(kt + 1) * 128,
                                   tt * 128:(tt + 1) * 128],
                        nt2[:, :])
                PS = pm_b.acquire()
                for kt in range(8):
                    C("tensor", lambda e, at=at, kt=kt: e.transpose(
                        PTF, at[:, kt * 128:(kt + 1) * 128], idn_f[:, :]),
                        r=[at[:, kt * 128:(kt + 1) * 128], idn_f[:, :]], w=[PTF])
                    wt2 = wT_b.acquire()
                    C("vector", lambda e, wt2=wt2: e.tensor_copy(wt2[:, :], PTF),
                        r=[PTF], w=[wt2[:, :]])
                    C("tensor", lambda e, kt=kt, wt2=wt2, PS=PS: e.matmul(
                        PS[:, 0:8], wt2[:, :], centT_sb[:, kt * 8:(kt + 1) * 8],
                        start=(kt == 0), stop=(kt == 7)),
                        r=[wt2[:, :], centT_sb[:, kt * 8:(kt + 1) * 8]],
                        w=[PS[:, :]])
                st0 = scm_b.acquire()
                C("vector", lambda e, st0=st0, PS=PS: e.tensor_copy(
                    st0[:, :], PS[:, 0:8]), r=[PS[:, :]], w=[st0[:, :]])
                sc_tiles.append(st0)

            COLL("AllGather", OP.bypass, GROUP8,
                 nh_in[par].ap(), nh_out[par].ap())

            mmarks = []
            for tt in range(4):
                st_ = sc_tiles[tt]
                tts = sct_b.acquire()
                C("vector", lambda e, tts=tts, st_=st_: e.tensor_reduce(
                    col(tts, 0), st_[:, :], axis=AX.X, op=OP.max),
                    r=[st_[:, :]], w=[col(tts, 0)])
                mb_ = (par * 4 + tt) * 8
                mcols = msk_sb[:, mb_:mb_ + 8]
                C("vector", lambda e, st_=st_, tts=tts, m=mcols: e.tensor_tensor(
                    m, st_[:, :], col(tts, 0).broadcast_to((128, 8)), op=OP.is_ge),
                    r=[st_[:, :], col(tts, 0)], w=[mcols])
                C("vector", lambda e, tts=tts, m=mcols: e.tensor_scalar(
                    tts[:, 8:16], m, -1.0, 1.0, op0=OP.mult, op1=OP.add),
                    r=[mcols], w=[tts[:, 8:16]])
                C("vector", lambda e, tts=tts: e.tensor_tensor_scan(
                    tts[:, 8:16], tts[:, 8:16], ones8_sb[:, :], 1.0,
                    op0=OP.mult, op1=OP.mult),
                    r=[tts[:, 8:16], ones8_sb[:, :]], w=[tts[:, 8:16]])
                shm = C("vector", lambda e, tts=tts, mb_=mb_: e.tensor_tensor(
                    msk_sb[:, mb_ + 1:mb_ + 8], msk_sb[:, mb_ + 1:mb_ + 8],
                    tts[:, 8:15], op=OP.mult),
                    r=[mcols, tts[:, 8:16]], w=[msk_sb[:, mb_ + 1:mb_ + 8]])
                C("vector", lambda e, st_=st_, tts=tts, m=mcols: e.tensor_tensor(
                    tts[:, 8:16], m, st_[:, :], op=OP.mult),
                    r=[mcols, st_[:, :]], w=[tts[:, 8:16]])
                C("vector", lambda e, tts=tts: e.tensor_reduce(
                    col(tts, 1), tts[:, 8:16], axis=AX.X, op=OP.add),
                    r=[tts[:, 8:16]], w=[col(tts, 1)])
                sgm = C("scalar", lambda e, tts=tts, tt=tt, par=par: e.activation(
                    col(alf_sb, par * 4 + tt), col(tts, 1), AF.Sigmoid),
                    r=[col(tts, 1)], w=[col(alf_sb, par * 4 + tt)])
                mmarks.append((shm, sgm))
            msk_done[sc] = mmarks

        bias1_ready = [None] * 32

        def stage_D(sc):
            par = sc
            for tc in range(8):
                for kt in range(8):
                    DMA("sync", nhD[kt][:, :],
                        nh_out[par][tc * DM + kt * 128:
                                    tc * DM + (kt + 1) * 128, :])
                    C("vector", lambda e, kt=kt: e.tensor_scalar(
                        nhD[kt][:, :], nhD[kt][:, :], col(gam_sb, kt),
                        col(bet_sb, kt), op0=OP.mult, op1=OP.add),
                        r=[nhD[kt][:, :], col(gam_sb, kt), col(bet_sb, kt)],
                        w=[nhD[kt][:, :]])
                for hc in range(8):
                    ws = w1c[8 * (hc % 2):8 * (hc % 2) + 8]
                    for kt in range(8):
                        DMA("sync", ws[kt][:, :], w1bf_d[hc, kt, :, :])
                    if sc == 0 and tc == 0:
                        for sub in range(4):
                            ht = hc * 4 + sub
                            for kt in range(8):
                                C("tensor", lambda e, kt=kt, sub=sub, ws=ws:
                                    e.matmul(
                                        PC1, ws[kt][:, sub * 128:(sub + 1) * 128],
                                        col(betbf_sb, kt), start=(kt == 0),
                                        stop=(kt == 7)),
                                    r=[ws[kt][:, sub * 128:(sub + 1) * 128],
                                       col(betbf_sb, kt)],
                                    w=[PC1])
                            bias1_ready[ht] = C(
                                "vector", lambda e, ht=ht: e.tensor_tensor(
                                    col(bias1_sb, ht), PC1, col(b1_sb, ht),
                                    op=OP.add),
                                r=[PC1, col(b1_sb, ht)], w=[col(bias1_sb, ht)])
                    for sub in range(4):
                        ht = hc * 4 + sub
                        PH = pm_b.acquire()
                        for kt in range(8):
                            C("tensor", lambda e, kt=kt, sub=sub, PH=PH, ws=ws:
                                e.matmul(
                                    PH[:, :], ws[kt][:, sub * 128:(sub + 1) * 128],
                                    nhD[kt][:, :], start=(kt == 0), stop=(kt == 7)),
                                r=[ws[kt][:, sub * 128:(sub + 1) * 128],
                                   nhD[kt][:, :]],
                                w=[PH[:, :]])
                        C("scalar", lambda e, ht=ht, PH=PH: e.activation(
                            r1sb[ht][:, :], PH[:, :], AF.Relu,
                            bias=col(bias1_sb, ht)),
                            r=[PH[:, :], col(bias1_sb, ht)],
                            waits=[bias1_ready[ht]], w=[r1sb[ht][:, :]])
                for dh in range(2):
                    PWs = [pm_b.acquire() for _ in range(4)]
                    for q in range(4):
                        qs_ = w2h[8 * (q % 2):8 * (q % 2) + 8]
                        for i in range(8):
                            ht = q * 8 + i
                            DMA("sync", qs_[i][:, :],
                                w2bf_d[ht * 128:(ht + 1) * 128,
                                       dh * 512:(dh + 1) * 512])
                        for tt in range(4):
                            for i in range(8):
                                ht = q * 8 + i
                                C("tensor", lambda e, ht=ht, i=i, tt=tt,
                                    PWs=PWs, qs_=qs_:
                                    e.matmul(
                                        PWs[tt][:, :],
                                        r1sb[ht][:, tt * 128:(tt + 1) * 128],
                                        qs_[i][:, :], start=(ht == 0),
                                        stop=False),
                                    r=[r1sb[ht][:, :], qs_[i][:, :]],
                                    w=[PWs[tt][:, :]])
                    for tt in range(4):
                        PW = PWs[tt]
                        C("tensor", lambda e, PW=PW, dh=dh: e.matmul(
                            PW[:, :], ones1_sb[:, :],
                            b2r_sb[:, dh * 512:(dh + 1) * 512],
                            start=False, stop=True),
                            r=[ones1_sb[:, :], b2r_sb[:, dh * 512:(dh + 1) * 512]],
                            w=[PW[:, :]])
                        ot = o2_b.acquire()
                        C("scalar", lambda e, ot=ot, PW=PW: e.activation(
                            ot[:, :], PW[:, :], AF.Copy), r=[PW[:, :]], w=[ot[:, :]])
                        DMA("gpsimd",
                            f_in[par][tc * 512 + tt * 128:tc * 512 + (tt + 1) * 128,
                                      dh * 512:(dh + 1) * 512],
                            ot[:, :])

            COLL("AllGather", OP.bypass, GROUP8,
                 f_in[par].ap(), f_out[par].ap())

        out_marks = []

        def stage_E(sc):
            par = sc
            for tt in range(4):
                at = acc_b.acquire()
                for ex in range(NEXP):
                    st_ = sl_b.acquire()
                    GATHER(st_[:, :], f_out[par][:, :], col(fidx_sb, ex * 4 + tt))
                    mc = col(msk_sb, (par * 4 + tt) * 8 + ex)
                    if ex == 0:
                        C("vector", lambda e, at=at, st_=st_, mc=mc:
                          e.tensor_scalar(at[:, :], st_[:, :], mc, None,
                                          op0=OP.mult),
                          r=[st_[:, :], mc], waits=[msk_done[sc][tt][0]],
                          w=[at[:, :]])
                    else:
                        C("vector", lambda e, at=at, st_=st_, mc=mc:
                          e.scalar_tensor_tensor(at[:, :], st_[:, :], mc,
                                                 at[:, :], op0=OP.mult, op1=OP.add),
                          r=[st_[:, :], mc, at[:, :]], w=[at[:, :]])
                wt = wE_b.acquire()
                DMA("sync", wt[:, :], w_d[par, tt * 128:(tt + 1) * 128, :])
                ac = col(alf_sb, par * 4 + tt)
                C("vector", lambda e, at=at, ac=ac: e.tensor_scalar(
                    at[:, :], at[:, :], ac, None, op0=OP.mult),
                    r=[at[:, :], ac], waits=[msk_done[sc][tt][1]], w=[at[:, :]])
                C("vector", lambda e, at=at, wt=wt: e.tensor_tensor(
                    at[:, :], at[:, :], wt[:, :], op=OP.add),
                    r=[at[:, :], wt[:, :]], w=[at[:, :]])
                r_c, mb_c = ln_pipeline(at[:, :], DM)
                C("vector", lambda e, at=at, r=r_c, mb=mb_c: e.tensor_scalar(
                    at[:, :], at[:, :], r, mb, op0=OP.mult, op1=OP.add),
                    r=[at[:, :], r_c, mb_c], w=[at[:, :]])
                xt = x1E_b.acquire()
                DMA("sync", xt[:, :], x1_d[sc, tt * 128:(tt + 1) * 128, :])
                C("vector", lambda e, xt=xt, at=at: e.tensor_tensor(
                    xt[:, :], xt[:, :], at[:, :], op=OP.add),
                    r=[xt[:, :], at[:, :]], w=[xt[:, :]])
                out_marks.append(
                    DMA("out", x2o[sc, tt * 128:(tt + 1) * 128, :], xt[:, :]))

        startup()
        stage_A(0)
        stage_A(1)
        stage_B(0)
        stage_C(0)
        stage_A(2)
        stage_B(1)
        stage_D(0)
        stage_C(1)
        stage_A(3)
        stage_B(2)
        stage_D(1)
        stage_E(0)
        stage_C(2)
        stage_B(3)
        stage_D(2)
        stage_E(1)
        stage_C(3)
        stage_D(3)
        stage_E(2)
        stage_E(3)

        if debug:
            fence = ([(s, s.v) for p in (qs_p, qg_p, qo_p, cc_p) for s in p.sems]
                     + [(sV, sV.v), (sS, sS.v), (sT, sT.v), (sG, sG.v)])
            for (o, t) in dbg_pairs:
                DMA("gpsimd", o.ap(), t.ap(), fence)
            for (nm, ap) in dbg_sbuf:
                DMA("gpsimd", dbg_sbuf_outs[nm][:, :], ap, fence)

        O("gpsimd", lambda e: e.nop(),
          waits=[out_marks[-1]]
          + [(s, s.v) for p in (qo_p, qg_p) for s in p.sems])

        block = E(nc.Block())
        hw = {e: {} for e in ENGS}

        def emitter(eng_name):
            def body(eng):
                seen = hw[eng_name]
                for (fn, ws, inc, inc_n) in ops[eng_name]:
                    for (s, v) in ws:
                        k = id(s)
                        if seen.get(k, 0) >= v:
                            continue
                        seen[k] = v
                        eng.wait_ge(s.h, v)
                    inst = fn(eng)
                    if inc is not None and inst is not None:
                        inst.then_inc(inc.h, inc_n)
            return body

        block.sync(emitter("sync"))
        block.scalar(emitter("scalar"))
        block.vector(emitter("vector"))
        block.tensor(emitter("tensor"))
        block.gpsimd(emitter("gpsimd"))

    return nc


_NC_CACHE = {}


def kernel(**inputs):
    x0 = np.asarray(inputs["L0_attn_id_seq"], np.float32)
    in_w = np.asarray(inputs["L0_attn_id_in_w"], np.float32)
    conv_w = np.asarray(inputs["L0_attn_id_conv_w"], np.float32)
    conv_b = np.asarray(inputs["L0_attn_id_conv_b"], np.float32)
    xproj = np.asarray(inputs["L0_attn_id_xproj_w"], np.float32)
    dt_w = np.asarray(inputs["L0_attn_id_dt_w"], np.float32)
    dt_b = np.asarray(inputs["L0_attn_id_dt_b"], np.float32)
    A_log = np.asarray(inputs["L0_attn_id_A_log"], np.float32)
    Dv = np.asarray(inputs["L0_attn_id_D"], np.float32)
    out_w = np.asarray(inputs["L0_attn_id_out_w"], np.float32)
    lng = np.asarray(inputs["L0_attn_id_ln_g"], np.float32)
    lnb = np.asarray(inputs["L0_attn_id_ln_b"], np.float32)
    cent = np.asarray(inputs["L0_ffn_centroids"], np.float32)
    gam = np.asarray(inputs["L0_ffn_gamma"], np.float32)
    bet = np.asarray(inputs["L0_ffn_beta"], np.float32)
    W1 = np.asarray(inputs["L0_ffn_W1"], np.float32)
    b1 = np.asarray(inputs["L0_ffn_b1"], np.float32)
    W2 = np.asarray(inputs["L0_ffn_W2"], np.float32)
    b2 = np.asarray(inputs["L0_ffn_b2"], np.float32)

    if "main" not in _NC_CACHE:
        _NC_CACHE["main"] = build(debug=False)
    nc = _NC_CACHE["main"]

    in_maps = []
    in_wT = np.ascontiguousarray(in_w.T)
    centT_h = np.ascontiguousarray(cent.T)
    for c in range(NCORES):
        b, q = c // 4, c % 4
        dq = q
        dsl = slice(dq * DSL, (dq + 1) * DSL)
        x0arr = np.stack([x0[b, sc * 2048 + q * 512: sc * 2048 + (q + 1) * 512]
                          for sc in range(NSC)])
        bidx_t = np.zeros((32, 128), np.uint32)
        for tch in range(4):
            for dt in range(4):
                base = tch * 2 * DI + dq * DSL + dt * 128
                bidx_t[tch * 8 + dt] = base + np.arange(128)
                bidx_t[tch * 8 + 4 + dt] = base + DI + np.arange(128)
        fidx_t = np.zeros((32, 128), np.uint32)
        for ex in range(8):
            for tt in range(4):
                fidx_t[ex * 4 + tt] = ex * GA + c * TSC + tt * 128 + np.arange(128)
        im = dict(
            x0c=np.ascontiguousarray(x0arr), in_wT=in_wT, ln_g=lng, ln_b=lnb,
            convw_sl=np.ascontiguousarray(conv_w[dsl]),
            convb_sl=np.ascontiguousarray(conv_b[dsl]),
            xprojT_sl=np.ascontiguousarray(xproj[:, dsl].T),
            dt_wT_sl=np.ascontiguousarray(dt_w[dsl].T),
            dt_b_sl=np.ascontiguousarray(dt_b[dsl]),
            A_log_sl=np.ascontiguousarray(A_log[dsl]),
            D_sl=np.ascontiguousarray(Dv[dsl]),
            out_wT_sl=np.ascontiguousarray(out_w[:, dsl].T),
            centT=centT_h,
            gam_e=np.ascontiguousarray(gam[c]), bet_e=np.ascontiguousarray(bet[c]),
            b1_e=np.ascontiguousarray(b1[c]), b2_e=np.ascontiguousarray(b2[c]),
            W1T_e=np.ascontiguousarray(W1[c].T), W2T_e=np.ascontiguousarray(W2[c].T),
            bidx=bidx_t, fidx=fidx_t,
        )
        in_maps.append(im)

    res = run_bass_kernel_spmd(nc, in_maps, core_ids=list(range(NCORES)))

    out = np.empty((2, 8192, 1024), np.float32)
    for c in range(NCORES):
        b, q = c // 4, c % 4
        r = np.asarray(res.results[c]["x2o"])
        for sc in range(NSC):
            out[b, sc * 2048 + q * 512: sc * 2048 + (q + 1) * 512] = r[sc]
    return out
